# version 17
# baseline (speedup 1.0000x reference)
import contextlib
import numpy as np
import concourse.bass as bass
import concourse.mybir as mybir
from concourse.bass_utils import run_bass_kernel_spmd

F32 = mybir.dt.float32
BF16 = mybir.dt.bfloat16
AF = mybir.ActivationFunctionType
ALU = mybir.AluOpType
AX = mybir.AxisListType

T = 2048
D = 1024
HC = 8
NT = 16
TB = 4
E = 8
CAP = 769.0
EPS = 1e-5


def build_nc():
    nc = bass.Bass("TRN2", target_bir_lowering=False, debug=False)
    dt = nc.dram_tensor
    x_fm = dt("x_fm", [D, T], F32, kind="ExternalInput").ap()
    x_tm = dt("x_tm", [T, D], F32, kind="ExternalInput").ap()
    wq_t = dt("wq_t", [D, D], F32, kind="ExternalInput").ap()
    wk_t = dt("wk_t", [D, D], F32, kind="ExternalInput").ap()
    wv_t = dt("wv_t", [D, D], F32, kind="ExternalInput").ap()
    wg1_t = dt("wg1_t", [D, 64], F32, kind="ExternalInput").ap()
    wg2_t = dt("wg2_t", [64, D], F32, kind="ExternalInput").ap()
    wo_t = dt("wo_t", [D, D], F32, kind="ExternalInput").ap()
    wr_t = dt("wr_t", [D, E], F32, kind="ExternalInput").ap()
    we = dt("we", [E, D, D], F32, kind="ExternalInput").ap()
    bq = dt("bq", [1, D], F32, kind="ExternalInput").ap()
    bk = dt("bk", [1, D], F32, kind="ExternalInput").ap()
    bv = dt("bv", [1, D], F32, kind="ExternalInput").ap()
    bg1 = dt("bg1", [1, 64], F32, kind="ExternalInput").ap()
    bg2 = dt("bg2", [1, D], F32, kind="ExternalInput").ap()
    bo = dt("bo", [1, D], F32, kind="ExternalInput").ap()
    gam = dt("gam", [128, HC], F32, kind="ExternalInput").ap()
    bet = dt("bet", [128, HC], F32, kind="ExternalInput").ap()
    ltri = dt("ltri", [128, 128], F32, kind="ExternalInput").ap()
    ident = dt("ident", [128, 128], F32, kind="ExternalInput").ap()
    y = dt("y", [T, D], F32, kind="ExternalOutput").ap()

    with contextlib.ExitStack() as ctx:
        sb = lambda n, s, d: ctx.enter_context(nc.sbuf_tensor(n, s, d))
        sem = lambda n: ctx.enter_context(nc.semaphore(n))
        ar1 = sb("ar1", [128, NT * D], BF16)
        ar2 = sb("ar2", [128, NT * D], BF16)
        ar3 = sb("ar3", [128, NT * D], BF16)
        ar4 = sb("ar4", [128, NT * D], BF16)
        wg1_sb = sb("wg1_sb", [128, HC, 64], BF16)
        wg2_sb = sb("wg2_sb", [64, D], BF16)
        wr_hi = sb("wr_hi", [128, HC, E], BF16)
        wr_lo = sb("wr_lo", [128, HC, E], BF16)
        bias_a = sb("bias_a", [128, D], BF16)
        bias_b = sb("bias_b", [128, D], BF16)
        bq_sb = bias_a[0:1, :]
        bg2_sb = bias_a[32:33, :]
        bg1_sb = bias_a[64:65, 0:64]
        bk_sb = bias_b[0:1, :]
        bv_sb = bias_b[32:33, :]
        bo_sb = bias_b[64:65, :]
        gam_sb = sb("gam_sb", [128, HC], F32)
        bet_sb = sb("bet_sb", [128, HC], F32)
        ltri_sb = sb("ltri_sb", [128, 128], BF16)
        ident_bf = sb("ident_bf", [8, 8], BF16)
        ones_b = sb("ones_b", [128, 512], BF16)
        ones_f = sb("ones_f", [128, 128], F32)
        ones_c_b = sb("ones_c_b", [128, 1], BF16)
        ones_c_f = sb("ones_c_f", [128, 1], F32)
        eps_sb = sb("eps_sb", [128, 1], F32)
        fpool = sb("fpool", [128, 2048], F32)
        fpool2 = sb("fpool2", [128, 2048], F32)
        bpool1 = sb("bpool1", [128, 2, 2048], BF16)
        bpool2 = sb("bpool2", [128, 2, 2048], BF16)
        reps = sb("reps", [128, 2, 2048], BF16)
        scr_b = sb("scr_b", [128, 1024], BF16)
        tmpb = scr_b[:, 0:512]
        tmpb2 = scr_b[:, 512:1024]
        srow = sb("srow", [1, T], F32)
        qrow = sb("qrow", [1, T], F32)
        crow = sb("crow", [1, T], BF16)
        drow = sb("drow", [1, T], BF16)
        r_c = sb("r_c", [128, NT], F32)
        nmu_c = sb("nmu_c", [128, NT], F32)
        wrsum_t = sb("wrsum_t", [1, E], F32)
        wrsum = wrsum_t[0:1, :]
        wr_rep = sb("wr_rep", [128, E], F32)
        lraw = sb("lraw", [128, NT * E], F32)
        wr_sb = lraw[:, 0:64].rearrange("p (c e) -> p c e", c=HC)
        lc = sb("lc", [128, NT * E], F32)
        v0c = sb("v0c", [128, NT], F32)
        nv1c = sb("nv1c", [128, NT], F32)
        m2t = sb("m2t", [128, NT * E], F32)
        dsc = sb("dsc", [128, NT], F32)
        w0c = sb("w0c", [128, NT], F32)
        w1c = sb("w1c", [128, NT], F32)
        hbf = sb("hbf", [128, NT * E], BF16)
        totr = srow[0:1, 0:NT * E]
        offr = srow[0:1, NT * E:2 * NT * E]
        within = lc
        cntt = sb("cntt", [128, NT * E], F32)
        rtmp = sb("rtmp", [128, NT * E], F32)
        lcor = sb("lcor", [128, NT * E], F32)
        oh0 = lraw
        oh1 = lcor
        rtmp2 = m2t
        pos0 = v0c
        pos1 = nv1c
        kp0 = dsc
        kp1 = sb("kp1", [128, NT], F32)
        g0c = sb("g0c", [128, NT], F32)
        g1c = sb("g1c", [128, NT], F32)
        msb = within
        gh_sb = bpool2[0:64, 1, :]
        kv_sb = scr_b[:].rearrange("p (c e) -> p c e", c=HC)
        sinv_sb = sb("sinv_sb", [128, HC], F32)
        srow2 = srow
        qrow2 = qrow
        with nc.psum_tensor("PS", [128, 4096], F32) as PS:
            s_pe = sem("s_pe")
            s_dv = sem("s_dv")
            s_ac = sem("s_ac")
            s_gp = sem("s_gp")
            s_d1 = sem("s_d1")
            s_d2 = sem("s_d2")
            s_d3 = sem("s_d3")

            def bank(b, lo=0, hi=512):
                return PS[:, 512 * b + lo: 512 * b + hi]

            xn = ar1[:].rearrange("p (c t) -> p c t", c=HC)
            xhi = ar3[:].rearrange("p (c t) -> p c t", c=HC)
            xlo3 = ar2[:].rearrange("p (c t) -> p c t", c=HC)
            expK = ar2[:].rearrange("p (n d) -> p n d", n=NT)
            qs = ar2[:].rearrange("p (c t) -> p c t", c=HC)
            zg = ar2[:].rearrange("p (c t) -> p c t", c=HC)
            accB = ar2[:].rearrange("p (n d) -> p n d", n=NT)
            vtm = ar3[:].rearrange("p (n d) -> p n d", n=NT)
            obf = ar3[:].rearrange("p (c t) -> p c t", c=HC)
            accA = ar3[:].rearrange("p (n d) -> p n d", n=NT)
            wkv = ar4[:].rearrange("p (w c o) -> p w c o", w=2, c=HC)
            lraw3 = lraw[:].rearrange("p (n e) -> p n e", n=NT)
            lc3 = lc[:].rearrange("p (n e) -> p n e", n=NT)
            oh0_3 = oh0[:].rearrange("p (n e) -> p n e", n=NT)
            oh1_3 = oh1[:].rearrange("p (n e) -> p n e", n=NT)
            m2_3 = m2t[:].rearrange("p (n e) -> p n e", n=NT)
            rt3 = rtmp[:].rearrange("p (n e) -> p n e", n=NT)
            rt23 = rtmp2[:].rearrange("p (n e) -> p n e", n=NT)

            def program(me, eng):
                c = {"pe": 0, "dv": 0, "ac": 0, "gp": 0, "d1": 0, "d2": 0, "d3": 0}
                S = {"pe": s_pe, "dv": s_dv, "ac": s_ac, "gp": s_gp,
                     "d1": s_d1, "d2": s_d2, "d3": s_d3}
                OWN = {"d1": "sync", "d2": "gpsimd", "d3": "scalar"}

                def w(owner, ctr, val=None):
                    if me == owner:
                        eng.wait_ge(S[ctr], c[ctr] if val is None else val)

                def mm(out, l, r, start, stop, inc=False):
                    if me == "tensor":
                        i = eng.matmul(out, l, r, start=start, stop=stop)
                        if inc:
                            i.then_inc(S["pe"], 1)
                    if inc:
                        c["pe"] += 1
                    return c["pe"]

                def dve(name, *a, inc=True, **k):
                    if me == "vector":
                        eng.wait_ge(S["dv"], c["dv"])
                        getattr(eng, name)(*a, **k).then_inc(S["dv"], 1)
                    c["dv"] += 1
                    return c["dv"]

                def act(name, *a, inc=True, **k):
                    if me == "scalar":
                        eng.wait_ge(S["ac"], c["ac"])
                        getattr(eng, name)(*a, **k).then_inc(S["ac"], 1)
                    c["ac"] += 1
                    return c["ac"]

                def gps(name, *a, inc=True, **k):
                    if me == "gpsimd":
                        eng.wait_ge(S["gp"], c["gp"])
                        getattr(eng, name)(*a, **k).then_inc(S["gp"], 1)
                    c["gp"] += 1
                    return c["gp"]

                def dma(q, out, in_):
                    if me == OWN[q]:
                        eng.wait_ge(S[q], c[q])
                        eng.dma_start(out=out, in_=in_).then_inc(S[q], 16)
                    c[q] += 16
                    return c[q]

                dma("d1", wr_sb[:], wr_t.rearrange("(c p) e -> p c e", p=128))
                dma("d1", gam_sb[:], gam)
                t_consts1 = dma("d1", bet_sb[:], bet)
                gps("memset", ones_b[:], 1.0, inc=False)
                gps("memset", ones_f[:], 1.0, inc=False)
                gps("memset", ones_c_b[:], 1.0, inc=False)
                gps("memset", ones_c_f[:], 1.0, inc=False)
                t_const_gp = gps("memset", eps_sb[:], EPS)
                t_xbf = []
                t_stats_h = []
                t_sq_h = []
                t_xlo_h = []
                t_xf_h = []
                for h in range(HC):
                    qf = "d1" if h % 2 == 0 else "d3"
                    fbuf = fpool[:, :] if h % 2 == 0 else fpool2[:, :]
                    if h >= 2:
                        w(OWN[qf], "dv", t_xlo_h[h - 2])
                    t_xf_h.append(dma(qf, fbuf,
                                      x_fm[128 * h:128 * (h + 1), :]))
                    t_xbf.append(dma("d2", xhi[:, h, :],
                                     x_fm[128 * h:128 * (h + 1), :]))
                    w("scalar", "d2", t_xbf[h])
                    if h >= 2:
                        w("scalar", "pe", t_stats_h[h - 2])
                    t_sq_h.append(act("activation", bpool2[:, h % 2, :],
                                      xhi[:, h, :], AF.Square))
                    w("vector", qf, t_xf_h[h])
                    w("vector", "d2", t_xbf[h])
                    t_xlo_h.append(dve("tensor_tensor", xlo3[:, h, :],
                                       fbuf, xhi[:, h, :], ALU.subtract))
                    if h == 0:
                        w("tensor", "gp", t_const_gp)
                    w("tensor", "d2", t_xbf[h])
                    w("tensor", "ac", t_sq_h[h])
                    for tb in range(TB):
                        mm(bank(tb)[0:1, :], ones_c_b[:, 0:1],
                           xhi[:, h, 512 * tb:512 * (tb + 1)],
                           start=(h == 0), stop=(h == HC - 1))
                    for tb in range(TB):
                        mm(bank(4 + tb)[0:1, :], ones_c_b[:, 0:1],
                           bpool2[:, h % 2, 512 * tb:512 * (tb + 1)],
                           start=(h == 0), stop=(h == HC - 1), inc=(tb == TB - 1))
                    t_stats_h.append(c["pe"])
                t_stats_pe = c["pe"]
                t_xlo = c["dv"]
                t_wrhi = dma("d2", wr_hi[:], wr_t.rearrange("(c p) e -> p c e", p=128))
                dma("d2", ltri_sb[:], ltri)
                t_ltri = dma("d2", ident_bf[:], ident[0:8, 0:8])
                dma("d2", bq_sb, bq)
                dma("d2", bk_sb, bk)
                dma("d2", bv_sb, bv)
                dma("d2", bg1_sb, bg1)
                dma("d2", bg2_sb, bg2)
                t_bias = dma("d2", bo_sb, bo)
                t_wk = dma("d2", wkv[:, 0, :, :], wk_t.rearrange("(c p) o -> p c o", p=128))
                t_wv = dma("d2", wkv[:, 1, :, :], wv_t.rearrange("(c p) o -> p c o", p=128))
                dma("d2", wg1_sb[:], wg1_t.rearrange("(c p) o -> p c o", p=128))
                t_wg = dma("d2", wg2_sb[:], wg2_t)
                w("vector", "gp", t_const_gp)
                w("vector", "pe", t_stats_pe)
                for tb in range(TB):
                    dve("tensor_copy", srow[:, 512 * tb:512 * (tb + 1)],
                        bank(tb)[0:1, :], inc=False)
                for tb in range(TB):
                    dve("tensor_copy", qrow[:, 512 * tb:512 * (tb + 1)],
                        bank(4 + tb)[0:1, :], inc=(tb == TB - 1))
                t_stats_ev = c["dv"]
                w("vector", "d1", t_consts1)
                w("vector", "d2", t_wrhi)
                t_wrlo = dve("tensor_tensor",
                             wr_lo[:].rearrange("p c e -> p (c e)"),
                             wr_sb[:].rearrange("p c e -> p (c e)"),
                             wr_hi[:].rearrange("p c e -> p (c e)"), ALU.subtract)
                w("tensor", "dv", t_wrlo)
                w("tensor", "d2", t_wrhi)
                for h in range(HC):
                    for tb in range(TB):
                        sl = slice(512 * tb, 512 * (tb + 1))
                        mm(bank(tb)[0:8, :], wr_hi[:, h, :], xhi[:, h, sl],
                           start=(h == 0), stop=False)
                        mm(bank(tb)[0:8, :], wr_lo[:, h, :], xhi[:, h, sl],
                           start=False, stop=False)
                        mm(bank(tb)[0:8, :], wr_hi[:, h, :], xlo3[:, h, sl],
                           start=False, stop=(h == HC - 1), inc=(tb == TB - 1))
                t_rt_pe = c["pe"]
                dve("tensor_scalar_mul", crow[:], srow[:], 1.0 / D, inc=False)
                t_mu1 = dve("tensor_scalar_mul", drow[:], qrow[:], 1.0 / D)
                w("scalar", "dv", t_mu1)
                w("scalar", "gp", t_const_gp)
                t_mu2 = act("activation", qrow[:], crow[:], AF.Square)
                w("vector", "ac", t_mu2)
                t_var = dve("tensor_tensor", srow[:], drow[:], qrow[:], ALU.subtract)
                w("scalar", "dv", t_var)
                t_sd = act("activation", qrow[:], srow[:], AF.Sqrt, bias=eps_sb[0:1, 0:1])
                w("vector", "ac", t_sd)
                dve("reciprocal", srow[:], qrow[:], inc=False)
                t_r_bf = dve("tensor_copy", drow[:], srow[:])
                w("tensor", "dv", t_r_bf)
                for tc in range(NT):
                    mm(bank(4)[:, tc:tc + 1], drow[0:1, 128 * tc:128 * (tc + 1)],
                       ones_b[0:1, 0:1], start=True, stop=True)
                for tc in range(NT):
                    mm(bank(4)[:, NT + tc:NT + tc + 1], crow[0:1, 128 * tc:128 * (tc + 1)],
                       ones_b[0:1, 0:1], start=True, stop=True, inc=(tc == NT - 1))
                t_cols_pe = c["pe"]
                dve("tensor_mul", qrow[:], crow[:], drow[:], inc=False)
                w("vector", "pe", t_cols_pe)
                t_rows = dve("tensor_scalar_mul", crow[:], qrow[:], -1.0)
                dve("tensor_copy", r_c[:], bank(4)[:, 0:NT], inc=False)
                t_cols_ev = dve("tensor_scalar", nmu_c[:], bank(4)[:, NT:2 * NT],
                                -1.0, None, op0=ALU.mult)

                t_reps_toks = []
                for which, rowsrc in ((0, drow), (1, crow)):
                    for tb in range(TB):
                        if which == 1 and tb == 0:
                            w("tensor", "dv", t_rows)
                        if t_reps_toks:
                            w("tensor", "dv", t_reps_toks[-1])
                        mm(bank(7)[:, :], ones_b[0:1, 0:128],
                           rowsrc[0:1, 512 * tb:512 * (tb + 1)], start=True, stop=True,
                           inc=True)
                        tk = c["pe"]
                        w("vector", "pe", tk)
                        t_reps_toks.append(dve("tensor_copy",
                                               reps[:, which, 512 * tb:512 * (tb + 1)],
                                               bank(7)[:, :]))
                t_reps = c["dv"]
                lgt_f = fpool[0:8, :]
                lgt_hi = bpool1[0:8, 0, :]
                lgt_lo = bpool1[0:8, 1, :]
                w("vector", "pe", t_rt_pe)
                for tb in range(TB):
                    dve("tensor_copy", lgt_f[:, 512 * tb:512 * (tb + 1)],
                        bank(tb)[0:8, :], inc=(tb == TB - 1))
                t_lgt_ev = c["dv"]
                dve("tensor_copy", lgt_hi[:, :], lgt_f[:, :], inc=False)
                t_lgt_hl = dve("tensor_tensor", lgt_lo[:, :], lgt_f[:, :], lgt_hi[:, :],
                               ALU.subtract)
                t_xn_tb = []
                w("vector", "dv", t_reps)
                for tb in range(TB):
                    sl = slice(512 * tb, 512 * (tb + 1))
                    for h in range(HC):
                        dve("tensor_tensor", tmpb[:, :], xhi[:, h, sl],
                            reps[:, 0, sl], ALU.mult, inc=False)
                        dve("tensor_tensor", xn[:, h, sl], tmpb[:, :],
                            reps[:, 1, sl], ALU.add, inc=(h == HC - 1))
                    t_xn_tb.append(c["dv"])
                t_xn = c["dv"]
                w("gpsimd", "dv", t_xn)
                t_kvz = gps("memset", scr_b[:, :], 0.0)

                w("tensor", "dv", t_lgt_hl)
                w("tensor", "d2", t_ltri)
                w("tensor", "dv", t_cols_ev)
                for tc in range(NT):
                    mm(bank(4)[:, 8 * tc:8 * (tc + 1)],
                       lgt_hi[:, 128 * tc:128 * (tc + 1)], ident_bf[0:8, 0:8],
                       start=(tc == 0), stop=False)
                    mm(bank(4)[:, 8 * tc:8 * (tc + 1)],
                       lgt_lo[:, 128 * tc:128 * (tc + 1)], ident_bf[0:8, 0:8],
                       start=False, stop=(tc == NT - 1), inc=(tc == NT - 1))
                t_lrawT_pe = c["pe"]
                w("vector", "pe", t_lrawT_pe)
                t_lraw_ev = dve("tensor_copy", lraw[:], bank(4)[:, 0:128])
                for h in range(HC):
                    mm(bank(6)[0:1, 0:E], ones_c_b[:, 0:1], wr_hi[:, h, :],
                       start=(h == 0), stop=False)
                    mm(bank(6)[0:1, 0:E], ones_c_b[:, 0:1], wr_lo[:, h, :],
                       start=False, stop=(h == HC - 1), inc=(h == HC - 1))
                t_wrsum_pe = c["pe"]
                w("vector", "pe", t_wrsum_pe)
                t_wr_ev = dve("tensor_copy", wrsum[:, :], bank(6)[0:1, 0:E])
                w("tensor", "dv", t_wr_ev)
                w("tensor", "dv", t_lraw_ev)
                mm(bank(4)[:, 0:E], ones_f[0:1, 0:128], wrsum[:, :], start=True,
                   stop=True, inc=True)
                t_wrr_pe = c["pe"]
                w("vector", "pe", t_wrr_pe)
                t_wrr_ev = dve("tensor_copy", wr_rep[:], bank(4)[:, 0:E])
                for tc in range(NT):
                    dve("scalar_tensor_tensor", lcor[:, 8 * tc:8 * (tc + 1)],
                        wr_rep[:], nmu_c[:, tc:tc + 1], lraw[:, 8 * tc:8 * (tc + 1)],
                        op0=ALU.mult, op1=ALU.add, inc=(tc == NT - 1))
                t_lcor = c["dv"]
                dve("tensor_tensor", lc3[:, :, :],
                    lcor[:].rearrange("p (n e) -> p n e", n=NT),
                    r_c[:].rearrange("p (n o) -> p n o", o=1).broadcast_to([128, NT, E]),
                    ALU.mult, inc=False)
                dve("reduce_max", v0c[:], lc3[:, :, :], axis=AX.X, inc=False)
                dve("tensor_tensor", oh0_3[:, :, :], lc3[:, :, :],
                    v0c[:].rearrange("p (n o) -> p n o", o=1).broadcast_to([128, NT, E]),
                    ALU.is_equal, inc=False)
                dve("scalar_tensor_tensor", m2_3[:, :, :], oh0_3[:, :, :], 1e9,
                    lc3[:, :, :], op0=ALU.mult, op1=ALU.subtract, inc=False)
                dve("tensor_reduce", nv1c[:], m2_3[:, :, :], AX.X, ALU.min, inc=False)
                dve("tensor_tensor", oh1_3[:, :, :], m2_3[:, :, :],
                    nv1c[:].rearrange("p (n o) -> p n o", o=1).broadcast_to([128, NT, E]),
                    ALU.is_equal, inc=False)
                t_d = dve("tensor_tensor", dsc[:], v0c[:], nv1c[:], ALU.add)
                w("gpsimd", "dv", t_r_bf)
                t_offz = gps("memset", offr[0:1, 0:E], 0.0)
                t_H = dve("tensor_tensor", hbf[:], oh0[:], oh1[:], ALU.add)

                tok_k = {}
                tok_v = {}
                for tc in range(NT):
                    for ob in range(2):
                        idx = tc * 2 + ob
                        bnk = idx % 4
                        if idx == 0:
                            w("tensor", "dv", t_lgt_ev)
                            w("tensor", "d2", t_wk)
                        w("tensor", "dv", t_xn_tb[tc // 4])
                        if idx >= 4:
                            w("tensor", "ac", tok_k[idx - 4])
                        for h in range(HC):
                            mm(bank(bnk), xn[:, h, 128 * tc:128 * (tc + 1)],
                               wkv[:, 0, h, 512 * ob:512 * (ob + 1)],
                               start=(h == 0), stop=False)
                        mm(bank(bnk), ones_b[0:1, 0:128],
                           bk_sb[:, 512 * ob:512 * (ob + 1)], start=False,
                           stop=True, inc=True)
                        tk = c["pe"]
                        w("scalar", "pe", tk)
                        tok_k[idx] = act("activation",
                                         expK[:, tc, 512 * ob:512 * (ob + 1)],
                                         bank(bnk), AF.Exp)
                t_k_pe = c["pe"]
                t_k_ac = c["ac"]

                w("scalar", "dv", t_d)
                t_w0 = act("activation", w0c[:], dsc[:], AF.Sigmoid)
                w("vector", "ac", t_w0)
                dve("tensor_scalar", w1c[:], w0c[:], 1.0, -1.0, op0=ALU.subtract,
                    op1=ALU.mult, inc=False)
                w("tensor", "dv", t_H)
                mm(bank(5)[:, 0:128], ltri_sb[:], hbf[:], start=True, stop=True)
                mm(bank(6)[0:1, 0:128], ones_c_b[:, 0:1], hbf[:], start=True, stop=True,
                   inc=True)
                t_within_pe = c["pe"]
                w("vector", "pe", t_within_pe)
                t_win_ev = dve("tensor_copy", within[:], bank(5)[:, 0:128])
                w("vector", "gp", t_offz)
                dve("tensor_copy", totr[:], bank(6)[0:1, 0:128], inc=False)
                for tc in range(1, NT):
                    dve("tensor_tensor", offr[0:1, 8 * tc:8 * (tc + 1)],
                        offr[0:1, 8 * (tc - 1):8 * tc], totr[0:1, 8 * (tc - 1):8 * tc],
                        ALU.add, inc=(tc == NT - 1))
                t_off = c["dv"]

                t_offrep_pe = None
                for tc in range(NT):
                    for ob in range(2):
                        idx = tc * 2 + ob
                        bnk = idx % 4
                        if idx == 0:
                            w("tensor", "d2", t_wv)
                        if idx == 8:
                            w("tensor", "dv", t_off)
                            mm(bank(6)[:, 0:128], ones_f[0:1, 0:128], offr[0:1, :],
                               start=True, stop=True, inc=True)
                            t_offrep_pe = c["pe"]
                        if idx >= 4:
                            w("tensor", "dv", tok_v[idx - 4])
                        else:
                            w("tensor", "ac", tok_k[28 + idx])
                        for h in range(HC):
                            mm(bank(bnk), xn[:, h, 128 * tc:128 * (tc + 1)],
                               wkv[:, 1, h, 512 * ob:512 * (ob + 1)],
                               start=(h == 0), stop=False)
                        mm(bank(bnk), ones_b[32:33, 0:128],
                           bv_sb[:, 512 * ob:512 * (ob + 1)], start=False,
                           stop=True, inc=True)
                        tk = c["pe"]
                        w("vector", "pe", tk)
                        tok_v[idx] = dve("tensor_copy",
                                         vtm[:, tc, 512 * ob:512 * (ob + 1)],
                                         bank(bnk))
                t_kv_pe = c["pe"]
                t_v_dv = c["dv"]
                w("vector", "pe", t_offrep_pe)
                dve("tensor_tensor", cntt[:], within[:], bank(6)[:, 0:128], ALU.add,
                    inc=False)
                dve("tensor_tensor", rtmp[:], oh0[:], cntt[:], ALU.mult, inc=False)
                dve("reduce_sum", pos0[:], rt3[:, :, :], axis=AX.X, inc=False)
                dve("tensor_tensor", rtmp2[:], oh1[:], cntt[:], ALU.mult, inc=False)
                dve("reduce_sum", pos1[:], rt23[:, :, :], axis=AX.X, inc=False)
                dve("tensor_single_scalar", kp0[:], pos0[:], CAP, ALU.is_lt, inc=False)
                dve("tensor_single_scalar", kp1[:], pos1[:], CAP, ALU.is_lt, inc=False)
                dve("tensor_tensor", g0c[:], w0c[:], kp0[:], ALU.mult, inc=False)
                dve("tensor_tensor", g1c[:], w1c[:], kp1[:], ALU.mult, inc=False)
                dve("tensor_tensor", rt3[:, :, :], oh0_3[:, :, :],
                    g0c[:].rearrange("p (n o) -> p n o", o=1).broadcast_to([128, NT, E]),
                    ALU.mult, inc=False)
                dve("tensor_tensor", rt23[:, :, :], oh1_3[:, :, :],
                    g1c[:].rearrange("p (n o) -> p n o", o=1).broadcast_to([128, NT, E]),
                    ALU.mult, inc=False)
                t_m = dve("tensor_tensor", msb[:], rtmp[:], rtmp2[:], ALU.add)

                kv_ev = {}
                for ch in range(HC):
                    b = 4 + ch % 2
                    b2 = 6 + ch % 2
                    if ch == 0:
                        w("tensor", "ac", t_k_ac)
                        w("tensor", "dv", t_v_dv)
                        w("tensor", "dv", t_m)
                    if ch >= 2:
                        w("tensor", "dv", kv_ev[ch - 2])
                    for tc in range(NT):
                        mm(bank(b, 0, 128), expK[:, tc, 128 * ch:128 * (ch + 1)],
                           vtm[:, tc, 128 * ch:128 * (ch + 1)], start=(tc == 0),
                           stop=(tc == NT - 1))
                    for tc in range(NT):
                        mm(bank(b2, 0, 1), expK[:, tc, 128 * ch:128 * (ch + 1)],
                           ones_c_b[:, 0:1], start=(tc == 0), stop=(tc == NT - 1),
                           inc=(tc == NT - 1))
                    tk = c["pe"]
                    w("vector", "pe", tk)
                    if ch == 0:
                        w("vector", "gp", t_kvz)
                    dve("reciprocal", sinv_sb[:, ch:ch + 1], bank(b2, 0, 1), inc=False)
                    if ch >= 1:
                        bp = 4 + (ch - 1) % 2
                        dve("tensor_scalar", kv_sb[0:64, ch - 1, 0:64],
                            bank(bp, 0, 128)[0:64, 0:64],
                            sinv_sb[0:64, ch - 1:ch], None, op0=ALU.mult, inc=False)
                        kv_ev[ch - 1] = dve("tensor_scalar",
                                            kv_sb[64:128, ch - 1, 64:128],
                                            bank(bp, 0, 128)[64:128, 64:128],
                                            sinv_sb[64:128, ch - 1:ch], None,
                                            op0=ALU.mult)
                bp = 4 + (HC - 1) % 2
                dve("engine_nop", inc=False)
                dve("tensor_scalar", kv_sb[0:64, HC - 1, 0:64],
                    bank(bp, 0, 128)[0:64, 0:64], sinv_sb[0:64, HC - 1:HC], None,
                    op0=ALU.mult, inc=False)
                kv_ev[HC - 1] = dve("tensor_scalar", kv_sb[64:128, HC - 1, 64:128],
                                    bank(bp, 0, 128)[64:128, 64:128],
                                    sinv_sb[64:128, HC - 1:HC], None, op0=ALU.mult)
                t_kv3 = c["dv"]

                w("gpsimd", "pe", t_k_pe)
                t_wq = dma("d2", wkv[:, 0, :, :], wq_t.rearrange("(c p) o -> p c o", p=128))
                w("gpsimd", "pe", t_kv_pe)
                t_wo = dma("d2", wkv[:, 1, :, :], wo_t.rearrange("(c p) o -> p c o", p=128))
                q_ev = {}
                for oc in range(HC):
                    for tb in range(TB):
                        idx = oc * TB + tb
                        bnk = idx % 4
                        if idx == 0:
                            w("tensor", "d2", t_wq)
                            w("tensor", "dv", tok_v[28])
                            w("tensor", "dv", tok_v[31])
                        if idx >= 4:
                            w("tensor", "ac", q_ev[idx - 4])
                        for h in range(HC):
                            mm(bank(bnk), wkv[:, 0, h, 128 * oc:128 * (oc + 1)],
                               xn[:, h, 512 * tb:512 * (tb + 1)], start=(h == 0),
                               stop=False)
                        mm(bank(bnk), bq_sb[:, 128 * oc:128 * (oc + 1)],
                           ones_b[0:1, 0:512], start=False, stop=True, inc=True)
                        tk = c["pe"]
                        w("scalar", "pe", tk)
                        q_ev[idx] = act("activation", qs[:, oc, 512 * tb:512 * (tb + 1)],
                                        bank(bnk), AF.Silu)
                t_q_pe = c["pe"]
                t_q_ac = c["ac"]
                g_ev = {}
                for tb in range(TB):
                    bnk = 4 + tb % 2
                    if tb == 0:
                        w("tensor", "dv", t_kv3)
                        w("tensor", "d2", t_wg)
                    if tb >= 2:
                        w("tensor", "dv", g_ev[tb - 2])
                    for h in range(HC):
                        mm(bank(bnk)[0:64, :], wg1_sb[:, h, :],
                           xn[:, h, 512 * tb:512 * (tb + 1)], start=(h == 0), stop=False)
                    mm(bank(bnk)[0:64, :], bg1_sb[:, 0:64], ones_b[64:65, 0:512],
                       start=False, stop=True, inc=True)
                    tk = c["pe"]
                    w("vector", "pe", tk)
                    g_ev[tb] = dve("tensor_copy", gh_sb[:, 512 * tb:512 * (tb + 1)],
                                   bank(bnk)[0:64, :])
                t_gh = c["dv"]

                o_ev = {}
                for ch in range(HC):
                    for tb in range(TB):
                        idx = ch * TB + tb
                        bnk = idx % 4
                        if idx == 0:
                            w("tensor", "ac", t_q_ac)
                            w("tensor", "dv", t_gh)
                        if idx >= 4:
                            w("tensor", "dv", o_ev[idx - 4])
                        mm(bank(bnk), kv_sb[:, ch, :], qs[:, ch, 512 * tb:512 * (tb + 1)],
                           start=True, stop=True, inc=True)
                        tk = c["pe"]
                        w("vector", "pe", tk)
                        o_ev[idx] = dve("tensor_copy", obf[:, ch, 512 * tb:512 * (tb + 1)],
                                        bank(bnk))
                t_o = c["dv"]

                t_st2_h = []
                t_sq2_h = []
                for ch in range(HC):
                    if ch == 0:
                        w("scalar", "dv", t_o)
                    if ch >= 2:
                        w("scalar", "pe", t_st2_h[ch - 2])
                    t_sq2_h.append(act("activation", bpool1[:, ch % 2, :], obf[:, ch, :],
                                       AF.Square))
                    if ch == 0:
                        w("tensor", "dv", t_o)
                    w("tensor", "ac", t_sq2_h[ch])
                    for tb in range(TB):
                        mm(bank(tb)[0:1, :], ones_c_b[:, 0:1],
                           obf[:, ch, 512 * tb:512 * (tb + 1)], start=(ch == 0),
                           stop=(ch == HC - 1))
                    for tb in range(TB):
                        mm(bank(4 + tb)[0:1, :], ones_c_b[:, 0:1],
                           bpool1[:, ch % 2, 512 * tb:512 * (tb + 1)], start=(ch == 0),
                           stop=(ch == HC - 1), inc=(tb == TB - 1))
                    t_st2_h.append(c["pe"])
                t_st2_pe = c["pe"]
                w("vector", "pe", t_st2_pe)
                for tb in range(TB):
                    dve("tensor_copy", srow2[:, 512 * tb:512 * (tb + 1)],
                        bank(tb)[0:1, :], inc=False)
                for tb in range(TB):
                    dve("tensor_copy", qrow2[:, 512 * tb:512 * (tb + 1)],
                        bank(4 + tb)[0:1, :], inc=(tb == TB - 1))
                t_st2_ev = c["dv"]
                dve("tensor_scalar_mul", crow[:], srow2[:], 1.0 / D, inc=False)
                t_mu1b = dve("tensor_scalar_mul", drow[:], qrow2[:], 1.0 / D)
                w("scalar", "dv", t_mu1b)
                t_mu2b = act("activation", qrow[:], crow[:], AF.Square)
                w("vector", "ac", t_mu2b)
                t_var2 = dve("tensor_tensor", srow[:], drow[:], qrow[:], ALU.subtract)
                w("scalar", "dv", t_var2)
                t_sd2 = act("activation", qrow[:], srow[:], AF.Sqrt, bias=eps_sb[0:1, 0:1])
                w("vector", "ac", t_sd2)
                dve("reciprocal", srow[:], qrow[:], inc=False)
                dve("tensor_copy", drow[:], srow[:], inc=False)
                dve("tensor_mul", qrow[:], crow[:], drow[:], inc=False)
                t_rows2 = dve("tensor_scalar_mul", crow[:], qrow[:], -1.0)
                w("tensor", "dv", t_rows2)
                t_reps2_toks = []
                for which, rowsrc in ((0, drow), (1, crow)):
                    for tb in range(TB):
                        if t_reps2_toks:
                            w("tensor", "dv", t_reps2_toks[-1])
                        mm(bank(7)[:, :], ones_b[0:1, 0:128],
                           rowsrc[0:1, 512 * tb:512 * (tb + 1)], start=True, stop=True,
                           inc=True)
                        tk = c["pe"]
                        w("vector", "pe", tk)
                        t_reps2_toks.append(dve("tensor_copy",
                                                reps[:, which,
                                                     512 * tb:512 * (tb + 1)],
                                                bank(7)[:, :]))
                t_reps2 = c["dv"]

                zg_ev = {}
                t_gate_ac = {}
                for oc in range(HC):
                    for tb in range(TB):
                        idx = oc * TB + tb
                        bnk = idx % 4
                        if idx == 0:
                            w("tensor", "dv", t_st2_ev)
                        if idx >= 4:
                            w("tensor", "dv", zg_ev[idx - 4])
                        mm(bank(bnk), wg2_sb[0:64, 128 * oc:128 * (oc + 1)],
                           gh_sb[:, 512 * tb:512 * (tb + 1)], start=True, stop=False)
                        mm(bank(bnk), bg2_sb[:, 128 * oc:128 * (oc + 1)],
                           ones_b[32:33, 0:512], start=False, stop=True, inc=True)
                        tk = c["pe"]
                        w("scalar", "pe", tk)
                        if idx >= 2:
                            w("scalar", "dv", zg_ev[idx - 2])
                        t_gate_ac[idx] = act("activation", bpool2[:, 0, 512 * (idx % 2):512 * (idx % 2) + 512],
                                             bank(bnk), AF.Sigmoid)
                        sl = slice(512 * tb, 512 * (tb + 1))
                        w("vector", "ac", t_gate_ac[idx])
                        if idx == 0:
                            w("vector", "dv", t_reps2)
                        dve("tensor_tensor", tmpb[:, :], obf[:, oc, sl],
                            reps[:, 0, sl], ALU.mult, inc=False)
                        dve("tensor_tensor", tmpb2[:, :], tmpb[:, :],
                            reps[:, 1, sl], ALU.add, inc=False)
                        dve("tensor_scalar", tmpb[:, :], tmpb2[:, :],
                            gam_sb[:, oc:oc + 1], bet_sb[:, oc:oc + 1], op0=ALU.mult,
                            op1=ALU.add, inc=False)
                        zg_ev[idx] = dve("tensor_tensor", zg[:, oc, sl], tmpb[:, :],
                                         bpool2[:, 0, 512 * (idx % 2):512 * (idx % 2) + 512],
                                         ALU.mult)
                t_zg = c["dv"]
                t_gate_ac_last = c["ac"]

                a_ev = {}
                for tc in range(NT):
                    for ob in range(2):
                        idx = tc * 2 + ob
                        bnk = idx % 4
                        if idx == 0:
                            w("tensor", "d2", t_wo)
                            w("tensor", "dv", t_zg)
                            w("tensor", "ac", t_gate_ac_last)
                        if idx >= 4:
                            w("tensor", "dv", a_ev[idx - 4])
                        for h in range(HC):
                            mm(bank(bnk), zg[:, h, 128 * tc:128 * (tc + 1)],
                               wkv[:, 1, h, 512 * ob:512 * (ob + 1)], start=(h == 0),
                               stop=False)
                        mm(bank(bnk), ones_b[64:65, 0:128],
                           bo_sb[:, 512 * ob:512 * (ob + 1)], start=False, stop=True,
                           inc=True)
                        tk = c["pe"]
                        w("vector", "pe", tk)
                        a_ev[idx] = dve("tensor_copy",
                                        accA[:, tc, 512 * ob:512 * (ob + 1)], bank(bnk))
                t_p5_pe = c["pe"]
                t_acc0 = c["dv"]

                pe_base = c["pe"]
                moe_pe = {e_i: pe_base + 32 * (e_i + 1) for e_i in range(E)}
                t_we = []
                for e_i in range(E):
                    if e_i == 0:
                        w("gpsimd", "pe", t_q_pe)
                    elif e_i == 1:
                        w("gpsimd", "pe", t_p5_pe)
                    else:
                        w("gpsimd", "pe", moe_pe[e_i - 2])
                    t_we.append(dma("d2", wkv[:, e_i % 2, :, :],
                                    we[e_i:e_i + 1, :, :].rearrange(
                                        "e (c p) f -> p (e c) f", p=128)))
                moe_ev = {}
                for e_i in range(E):
                    src = accA if e_i % 2 == 0 else accB
                    dstv = accB if e_i % 2 == 0 else accA
                    for tc in range(NT):
                        for ob in range(2):
                            gidx = e_i * NT * 2 + tc * 2 + ob
                            bnk = 4 + (tc * 2 + ob) % 4
                            w("tensor", "d2", t_we[e_i])
                            if gidx == 0:
                                w("tensor", "dv", t_acc0)
                            if gidx >= 4:
                                w("tensor", "dv", moe_ev[gidx - 4])
                            for h in range(HC):
                                mm(bank(bnk), xn[:, h, 128 * tc:128 * (tc + 1)],
                                   wkv[:, e_i % 2, h, 512 * ob:512 * (ob + 1)],
                                   start=(h == 0), stop=(h == HC - 1),
                                   inc=(h == HC - 1))
                            tk = c["pe"]
                            w("vector", "pe", tk)
                            moe_ev[gidx] = dve(
                                "scalar_tensor_tensor",
                                dstv[:, tc, 512 * ob:512 * (ob + 1)], bank(bnk),
                                msb[:, tc * E + e_i: tc * E + e_i + 1],
                                src[:, tc, 512 * ob:512 * (ob + 1)],
                                op0=ALU.mult, op1=ALU.add)
                assert c["pe"] == moe_pe[E - 1], (c["pe"], moe_pe)
                t_moe = c["dv"]

                pbs = (fpool[:, :], fpool2[:, :])
                out_toks = {}
                add_toks = {}
                w("sync", "dv", t_lgt_hl)
                for tc in range(NT):
                    pb = pbs[tc % 2]
                    if tc >= 2:
                        w("sync", "gp", add_toks[tc - 2])
                    tx = dma("d1", pb[:, 0:1024], x_tm[128 * tc:128 * (tc + 1), :])
                    w("gpsimd", "d1", tx)
                    w("gpsimd", "dv", moe_ev[7 * NT * 2 + tc * 2 + 1])
                    if tc >= 2:
                        w("gpsimd", "d3", out_toks[tc - 2])
                    add_toks[tc] = gps("tensor_tensor", pb[:, 1024:2048],
                                       accA[:, tc, :], pb[:, 0:1024], ALU.add)
                    w("scalar", "gp", add_toks[tc])
                    out_toks[tc] = dma("d3", y[128 * tc:128 * (tc + 1), :],
                                       pb[:, 1024:2048])
                w("scalar", "d3", out_toks[NT - 1])

            with nc.Block() as block:
                @block.tensor
                def _(eng):
                    program("tensor", eng)

                @block.vector
                def _(eng):
                    program("vector", eng)

                @block.scalar
                def _(eng):
                    program("scalar", eng)

                @block.sync
                def _(eng):
                    program("sync", eng)

                @block.gpsimd
                def _(eng):
                    program("gpsimd", eng)

    return nc


_NC_CACHE = None


def kernel(L0_attn_x, L0_attn_w_qkv, L0_attn_b_qkv, L0_attn_w_g1, L0_attn_b_g1,
           L0_attn_w_g2, L0_attn_b_g2, L0_attn_w_out, L0_attn_b_out, L0_attn_ln_g,
           L0_attn_ln_b, L0_ffn_Wr, L0_ffn_We):
    global _NC_CACHE
    x = np.asarray(L0_attn_x, np.float32)
    w_qkv = np.asarray(L0_attn_w_qkv, np.float32)
    b_qkv = np.asarray(L0_attn_b_qkv, np.float32)
    shared = {
        "wq_t": np.ascontiguousarray(w_qkv[0:D].T),
        "wk_t": np.ascontiguousarray(w_qkv[D:2 * D].T),
        "wv_t": np.ascontiguousarray(w_qkv[2 * D:3 * D].T),
        "wg1_t": np.ascontiguousarray(np.asarray(L0_attn_w_g1, np.float32).T),
        "wg2_t": np.ascontiguousarray(np.asarray(L0_attn_w_g2, np.float32).T),
        "wo_t": np.ascontiguousarray(np.asarray(L0_attn_w_out, np.float32).T),
        "wr_t": np.ascontiguousarray(np.asarray(L0_ffn_Wr, np.float32).T),
        "we": np.ascontiguousarray(np.asarray(L0_ffn_We, np.float32)),
        "bq": np.ascontiguousarray(b_qkv[0:D][None, :]),
        "bk": np.ascontiguousarray(b_qkv[D:2 * D][None, :]),
        "bv": np.ascontiguousarray(b_qkv[2 * D:3 * D][None, :]),
        "bg1": np.ascontiguousarray(np.asarray(L0_attn_b_g1, np.float32)[None, :]),
        "bg2": np.ascontiguousarray(np.asarray(L0_attn_b_g2, np.float32)[None, :]),
        "bo": np.ascontiguousarray(np.asarray(L0_attn_b_out, np.float32)[None, :]),
        "gam": np.ascontiguousarray(np.asarray(L0_attn_ln_g, np.float32).reshape(HC, 128).T),
        "bet": np.ascontiguousarray(np.asarray(L0_attn_ln_b, np.float32).reshape(HC, 128).T),
        "ltri": np.triu(np.ones((128, 128), np.float32), 1),
        "ident": np.eye(128, dtype=np.float32),
    }
    in_maps = []
    for b in range(8):
        m = dict(shared)
        m["x_fm"] = np.ascontiguousarray(x[b].T)
        m["x_tm"] = np.ascontiguousarray(x[b])
        in_maps.append(m)
    if _NC_CACHE is None:
        _NC_CACHE = build_nc()
    res = run_bass_kernel_spmd(_NC_CACHE, in_maps, core_ids=list(range(8)))
    return np.stack([np.asarray(r["y"], np.float32) for r in res.results], axis=0)


# version 18
# speedup vs baseline: 1.0087x; 1.0087x over previous
import contextlib
import numpy as np
import concourse.bass as bass
import concourse.mybir as mybir
from concourse.bass_utils import run_bass_kernel_spmd

F32 = mybir.dt.float32
BF16 = mybir.dt.bfloat16
AF = mybir.ActivationFunctionType
ALU = mybir.AluOpType
AX = mybir.AxisListType

T = 2048
D = 1024
HC = 8
NT = 16
TB = 4
E = 8
CAP = 769.0
EPS = 1e-5


def build_nc():
    nc = bass.Bass("TRN2", target_bir_lowering=False, debug=False)
    dt = nc.dram_tensor
    x_fm = dt("x_fm", [D, T], F32, kind="ExternalInput").ap()
    x_tm = dt("x_tm", [T, D], F32, kind="ExternalInput").ap()
    wq_t = dt("wq_t", [D, D], F32, kind="ExternalInput").ap()
    wk_t = dt("wk_t", [D, D], F32, kind="ExternalInput").ap()
    wv_t = dt("wv_t", [D, D], F32, kind="ExternalInput").ap()
    wg1_t = dt("wg1_t", [D, 64], F32, kind="ExternalInput").ap()
    wg2_t = dt("wg2_t", [64, D], F32, kind="ExternalInput").ap()
    wo_t = dt("wo_t", [D, D], F32, kind="ExternalInput").ap()
    wr_t = dt("wr_t", [D, E], F32, kind="ExternalInput").ap()
    we = dt("we", [E, D, D], F32, kind="ExternalInput").ap()
    bq = dt("bq", [1, D], F32, kind="ExternalInput").ap()
    bk = dt("bk", [1, D], F32, kind="ExternalInput").ap()
    bv = dt("bv", [1, D], F32, kind="ExternalInput").ap()
    bg1 = dt("bg1", [1, 64], F32, kind="ExternalInput").ap()
    bg2 = dt("bg2", [1, D], F32, kind="ExternalInput").ap()
    bo = dt("bo", [1, D], F32, kind="ExternalInput").ap()
    gam = dt("gam", [128, HC], F32, kind="ExternalInput").ap()
    bet = dt("bet", [128, HC], F32, kind="ExternalInput").ap()
    ltri = dt("ltri", [128, 128], F32, kind="ExternalInput").ap()
    ident = dt("ident", [128, 128], F32, kind="ExternalInput").ap()
    y = dt("y", [T, D], F32, kind="ExternalOutput").ap()

    with contextlib.ExitStack() as ctx:
        sb = lambda n, s, d: ctx.enter_context(nc.sbuf_tensor(n, s, d))
        sem = lambda n: ctx.enter_context(nc.semaphore(n))
        ar1 = sb("ar1", [128, NT * D], BF16)
        ar2 = sb("ar2", [128, NT * D], BF16)
        ar3 = sb("ar3", [128, NT * D], BF16)
        ar4 = sb("ar4", [128, NT * D], BF16)
        wg1_sb = sb("wg1_sb", [128, HC, 64], BF16)
        wg2_sb = sb("wg2_sb", [64, D], BF16)
        wr_hi = sb("wr_hi", [128, HC, E], BF16)
        wr_lo = sb("wr_lo", [128, HC, E], BF16)
        bias_a = sb("bias_a", [128, D], BF16)
        bias_b = sb("bias_b", [128, D], BF16)
        bq_sb = bias_a[0:1, :]
        bg2_sb = bias_a[32:33, :]
        bg1_sb = bias_a[64:65, 0:64]
        bk_sb = bias_b[0:1, :]
        bv_sb = bias_b[32:33, :]
        bo_sb = bias_b[64:65, :]
        gam_sb = sb("gam_sb", [128, HC], F32)
        bet_sb = sb("bet_sb", [128, HC], F32)
        ltri_sb = sb("ltri_sb", [128, 128], BF16)
        ident_bf = sb("ident_bf", [8, 8], BF16)
        ones_b = sb("ones_b", [128, 512], BF16)
        ones_f = sb("ones_f", [128, 128], F32)
        ones_c_b = sb("ones_c_b", [128, 1], BF16)
        ones_c_f = sb("ones_c_f", [128, 1], F32)
        eps_sb = sb("eps_sb", [128, 1], F32)
        fpool = sb("fpool", [128, 2048], F32)
        fpool2 = sb("fpool2", [128, 2048], F32)
        bpool1 = sb("bpool1", [128, 2, 2048], BF16)
        bpool2 = sb("bpool2", [128, 2, 2048], BF16)
        reps = sb("reps", [128, 2, 2048], BF16)
        scr_b = sb("scr_b", [128, 1024], BF16)
        tmpb = scr_b[:, 0:512]
        tmpb2 = scr_b[:, 512:1024]
        srow = sb("srow", [1, T], F32)
        qrow = sb("qrow", [1, T], F32)
        crow = sb("crow", [1, T], BF16)
        drow = sb("drow", [1, T], BF16)
        r_c = sb("r_c", [128, NT], F32)
        nmu_c = sb("nmu_c", [128, NT], F32)
        wrsum_t = sb("wrsum_t", [1, E], F32)
        wrsum = wrsum_t[0:1, :]
        wr_rep = sb("wr_rep", [128, E], F32)
        lraw = sb("lraw", [128, NT * E], F32)
        wr_sb = lraw[:, 0:64].rearrange("p (c e) -> p c e", c=HC)
        lc = sb("lc", [128, NT * E], F32)
        v0c = sb("v0c", [128, NT], F32)
        nv1c = sb("nv1c", [128, NT], F32)
        m2t = sb("m2t", [128, NT * E], F32)
        dsc = sb("dsc", [128, NT], F32)
        w0c = sb("w0c", [128, NT], F32)
        w1c = sb("w1c", [128, NT], F32)
        hbf = sb("hbf", [128, NT * E], BF16)
        totr = srow[0:1, 0:NT * E]
        offr = srow[0:1, NT * E:2 * NT * E]
        within = lc
        cntt = sb("cntt", [128, NT * E], F32)
        rtmp = sb("rtmp", [128, NT * E], F32)
        lcor = sb("lcor", [128, NT * E], F32)
        oh0 = lraw
        oh1 = lcor
        rtmp2 = m2t
        pos0 = v0c
        pos1 = nv1c
        kp0 = dsc
        kp1 = sb("kp1", [128, NT], F32)
        g0c = sb("g0c", [128, NT], F32)
        g1c = sb("g1c", [128, NT], F32)
        msb = within
        gh_sb = bpool2[0:64, 1, :]
        kv_sb = scr_b[:].rearrange("p (c e) -> p c e", c=HC)
        sinv_sb = sb("sinv_sb", [128, HC], F32)
        srow2 = srow
        qrow2 = qrow
        with nc.psum_tensor("PS", [128, 4096], F32) as PS:
            s_pe = sem("s_pe")
            s_dv = sem("s_dv")
            s_ac = sem("s_ac")
            s_gp = sem("s_gp")
            s_d1 = sem("s_d1")
            s_d2 = sem("s_d2")
            s_d3 = sem("s_d3")

            def bank(b, lo=0, hi=512):
                return PS[:, 512 * b + lo: 512 * b + hi]

            xn = ar1[:].rearrange("p (c t) -> p c t", c=HC)
            xhi = ar3[:].rearrange("p (c t) -> p c t", c=HC)
            xlo3 = ar2[:].rearrange("p (c t) -> p c t", c=HC)
            expK = ar2[:].rearrange("p (n d) -> p n d", n=NT)
            qs = ar2[:].rearrange("p (c t) -> p c t", c=HC)
            zg = ar2[:].rearrange("p (c t) -> p c t", c=HC)
            accB = ar2[:].rearrange("p (n d) -> p n d", n=NT)
            vtm = ar3[:].rearrange("p (n d) -> p n d", n=NT)
            obf = ar3[:].rearrange("p (c t) -> p c t", c=HC)
            accA = ar3[:].rearrange("p (n d) -> p n d", n=NT)
            wkv = ar4[:].rearrange("p (w c o) -> p w c o", w=2, c=HC)
            lraw3 = lraw[:].rearrange("p (n e) -> p n e", n=NT)
            lc3 = lc[:].rearrange("p (n e) -> p n e", n=NT)
            oh0_3 = oh0[:].rearrange("p (n e) -> p n e", n=NT)
            oh1_3 = oh1[:].rearrange("p (n e) -> p n e", n=NT)
            m2_3 = m2t[:].rearrange("p (n e) -> p n e", n=NT)
            rt3 = rtmp[:].rearrange("p (n e) -> p n e", n=NT)
            rt23 = rtmp2[:].rearrange("p (n e) -> p n e", n=NT)

            def program(me, eng):
                c = {"pe": 0, "dv": 0, "ac": 0, "gp": 0, "d1": 0, "d2": 0, "d3": 0}
                S = {"pe": s_pe, "dv": s_dv, "ac": s_ac, "gp": s_gp,
                     "d1": s_d1, "d2": s_d2, "d3": s_d3}
                OWN = {"d1": "sync", "d2": "gpsimd", "d3": "scalar"}

                def w(owner, ctr, val=None):
                    if me == owner:
                        eng.wait_ge(S[ctr], c[ctr] if val is None else val)

                def mm(out, l, r, start, stop, inc=False):
                    if me == "tensor":
                        i = eng.matmul(out, l, r, start=start, stop=stop)
                        if inc:
                            i.then_inc(S["pe"], 1)
                    if inc:
                        c["pe"] += 1
                    return c["pe"]

                def dve(name, *a, inc=True, **k):
                    if me == "vector":
                        eng.wait_ge(S["dv"], c["dv"])
                        getattr(eng, name)(*a, **k).then_inc(S["dv"], 1)
                    c["dv"] += 1
                    return c["dv"]

                def act(name, *a, inc=True, **k):
                    if me == "scalar":
                        eng.wait_ge(S["ac"], c["ac"])
                        getattr(eng, name)(*a, **k).then_inc(S["ac"], 1)
                    c["ac"] += 1
                    return c["ac"]

                def gps(name, *a, inc=True, **k):
                    if me == "gpsimd":
                        eng.wait_ge(S["gp"], c["gp"])
                        getattr(eng, name)(*a, **k).then_inc(S["gp"], 1)
                    c["gp"] += 1
                    return c["gp"]

                def dma(q, out, in_):
                    if me == OWN[q]:
                        eng.wait_ge(S[q], c[q])
                        eng.dma_start(out=out, in_=in_).then_inc(S[q], 16)
                    c[q] += 16
                    return c[q]

                dma("d1", wr_sb[:], wr_t.rearrange("(c p) e -> p c e", p=128))
                dma("d1", gam_sb[:], gam)
                t_consts1 = dma("d1", bet_sb[:], bet)
                gps("memset", ones_b[:], 1.0, inc=False)
                gps("memset", ones_f[:], 1.0, inc=False)
                gps("memset", ones_c_b[:], 1.0, inc=False)
                gps("memset", ones_c_f[:], 1.0, inc=False)
                t_const_gp = gps("memset", eps_sb[:], EPS)
                t_xbf = []
                t_stats_h = []
                t_sq_h = []
                t_xlo_h = []
                t_xf_h = []
                for h in range(HC):
                    qf = "d1" if h % 2 == 0 else "d3"
                    fbuf = fpool[:, :] if h % 2 == 0 else fpool2[:, :]
                    if h >= 2:
                        w(OWN[qf], "dv", t_xlo_h[h - 2])
                    t_xf_h.append(dma(qf, fbuf,
                                      x_fm[128 * h:128 * (h + 1), :]))
                    t_xbf.append(dma("d2", xhi[:, h, :],
                                     x_fm[128 * h:128 * (h + 1), :]))
                    w("scalar", "d2", t_xbf[h])
                    if h >= 2:
                        w("scalar", "pe", t_stats_h[h - 2])
                    t_sq_h.append(act("activation", bpool2[:, h % 2, :],
                                      xhi[:, h, :], AF.Square))
                    w("vector", qf, t_xf_h[h])
                    w("vector", "d2", t_xbf[h])
                    t_xlo_h.append(dve("tensor_tensor", xlo3[:, h, :],
                                       fbuf, xhi[:, h, :], ALU.subtract))
                    if h == 0:
                        w("tensor", "gp", t_const_gp)
                    w("tensor", "d2", t_xbf[h])
                    w("tensor", "ac", t_sq_h[h])
                    for tb in range(TB):
                        mm(bank(tb)[0:1, :], ones_c_b[:, 0:1],
                           xhi[:, h, 512 * tb:512 * (tb + 1)],
                           start=(h == 0), stop=(h == HC - 1))
                    for tb in range(TB):
                        mm(bank(4 + tb)[0:1, :], ones_c_b[:, 0:1],
                           bpool2[:, h % 2, 512 * tb:512 * (tb + 1)],
                           start=(h == 0), stop=(h == HC - 1), inc=(tb == TB - 1))
                    t_stats_h.append(c["pe"])
                t_stats_pe = c["pe"]
                t_xlo = c["dv"]
                t_wrhi = dma("d2", wr_hi[:], wr_t.rearrange("(c p) e -> p c e", p=128))
                dma("d2", ltri_sb[:], ltri)
                t_ltri = dma("d2", ident_bf[:], ident[0:8, 0:8])
                dma("d2", bq_sb, bq)
                dma("d2", bk_sb, bk)
                dma("d2", bv_sb, bv)
                dma("d2", bg1_sb, bg1)
                dma("d2", bg2_sb, bg2)
                t_bias = dma("d2", bo_sb, bo)
                t_wk = dma("d2", wkv[:, 0, :, :], wk_t.rearrange("(c p) o -> p c o", p=128))
                t_wv = dma("d2", wkv[:, 1, :, :], wv_t.rearrange("(c p) o -> p c o", p=128))
                dma("d2", wg1_sb[:], wg1_t.rearrange("(c p) o -> p c o", p=128))
                t_wg = dma("d2", wg2_sb[:], wg2_t)
                w("vector", "gp", t_const_gp)
                w("vector", "pe", t_stats_pe)
                for tb in range(TB):
                    dve("tensor_copy", srow[:, 512 * tb:512 * (tb + 1)],
                        bank(tb)[0:1, :], inc=False)
                for tb in range(TB):
                    dve("tensor_copy", qrow[:, 512 * tb:512 * (tb + 1)],
                        bank(4 + tb)[0:1, :], inc=(tb == TB - 1))
                t_stats_ev = c["dv"]
                w("vector", "d1", t_consts1)
                w("vector", "d2", t_wrhi)
                t_wrlo = dve("tensor_tensor",
                             wr_lo[:].rearrange("p c e -> p (c e)"),
                             wr_sb[:].rearrange("p c e -> p (c e)"),
                             wr_hi[:].rearrange("p c e -> p (c e)"), ALU.subtract)
                w("tensor", "dv", t_wrlo)
                w("tensor", "d2", t_wrhi)
                for h in range(HC):
                    for tb in range(TB):
                        sl = slice(512 * tb, 512 * (tb + 1))
                        mm(bank(tb)[0:8, :], wr_hi[:, h, :], xhi[:, h, sl],
                           start=(h == 0), stop=False)
                        mm(bank(tb)[0:8, :], wr_lo[:, h, :], xhi[:, h, sl],
                           start=False, stop=False)
                        mm(bank(tb)[0:8, :], wr_hi[:, h, :], xlo3[:, h, sl],
                           start=False, stop=(h == HC - 1), inc=(tb == TB - 1))
                t_rt_pe = c["pe"]
                dve("tensor_scalar_mul", crow[:], srow[:], 1.0 / D, inc=False)
                t_mu1 = dve("tensor_scalar_mul", drow[:], qrow[:], 1.0 / D)
                w("scalar", "dv", t_mu1)
                w("scalar", "gp", t_const_gp)
                t_mu2 = act("activation", qrow[:], crow[:], AF.Square)
                w("vector", "ac", t_mu2)
                t_var = dve("tensor_tensor", srow[:], drow[:], qrow[:], ALU.subtract)
                w("scalar", "dv", t_var)
                t_sd = act("activation", qrow[:], srow[:], AF.Sqrt, bias=eps_sb[0:1, 0:1])
                w("vector", "ac", t_sd)
                dve("reciprocal", srow[:], qrow[:], inc=False)
                t_r_bf = dve("tensor_copy", drow[:], srow[:])
                w("tensor", "dv", t_r_bf)
                for tc in range(NT):
                    mm(bank(4)[:, tc:tc + 1], drow[0:1, 128 * tc:128 * (tc + 1)],
                       ones_b[0:1, 0:1], start=True, stop=True)
                for tc in range(NT):
                    mm(bank(4)[:, NT + tc:NT + tc + 1], crow[0:1, 128 * tc:128 * (tc + 1)],
                       ones_b[0:1, 0:1], start=True, stop=True, inc=(tc == NT - 1))
                t_cols_pe = c["pe"]
                dve("tensor_mul", qrow[:], crow[:], drow[:], inc=False)
                w("vector", "pe", t_cols_pe)
                t_rows = dve("tensor_scalar_mul", crow[:], qrow[:], -1.0)
                dve("tensor_copy", r_c[:], bank(4)[:, 0:NT], inc=False)
                t_cols_ev = dve("tensor_scalar", nmu_c[:], bank(4)[:, NT:2 * NT],
                                -1.0, None, op0=ALU.mult)

                t_reps_toks = []
                for which, rowsrc in ((0, drow), (1, crow)):
                    for tb in range(TB):
                        if which == 1 and tb == 0:
                            w("tensor", "dv", t_rows)
                        if t_reps_toks:
                            w("tensor", "dv", t_reps_toks[-1])
                        mm(bank(7)[:, :], ones_b[0:1, 0:128],
                           rowsrc[0:1, 512 * tb:512 * (tb + 1)], start=True, stop=True,
                           inc=True)
                        tk = c["pe"]
                        w("vector", "pe", tk)
                        t_reps_toks.append(dve("tensor_copy",
                                               reps[:, which, 512 * tb:512 * (tb + 1)],
                                               bank(7)[:, :]))
                t_reps = c["dv"]
                lgt_f = fpool[0:8, :]
                lgt_hi = bpool1[0:8, 0, :]
                lgt_lo = bpool1[0:8, 1, :]
                w("vector", "pe", t_rt_pe)
                for tb in range(TB):
                    dve("tensor_copy", lgt_f[:, 512 * tb:512 * (tb + 1)],
                        bank(tb)[0:8, :], inc=(tb == TB - 1))
                t_lgt_ev = c["dv"]
                dve("tensor_copy", lgt_hi[:, :], lgt_f[:, :], inc=False)
                t_lgt_hl = dve("tensor_tensor", lgt_lo[:, :], lgt_f[:, :], lgt_hi[:, :],
                               ALU.subtract)
                t_xn_tb = []
                w("vector", "dv", t_reps)
                w("gpsimd", "dv", t_reps)
                for tb in range(TB):
                    sl = slice(512 * tb, 512 * (tb + 1))
                    op = dve if tb % 2 == 0 else gps
                    scr = tmpb if tb % 2 == 0 else tmpb2
                    for h in range(HC):
                        op("tensor_tensor", scr[:, :], xhi[:, h, sl],
                           reps[:, 0, sl], ALU.mult, inc=False)
                        op("tensor_tensor", xn[:, h, sl], scr[:, :],
                           reps[:, 1, sl], ALU.add, inc=(h == HC - 1))
                    t_xn_tb.append(("dv" if tb % 2 == 0 else "gp", c["dv" if tb % 2 == 0 else "gp"]))
                t_xn = c["dv"]
                t_xn_gp = c["gp"]
                w("gpsimd", "dv", t_xn)
                t_kvz = gps("memset", scr_b[:, :], 0.0)

                w("tensor", "dv", t_lgt_hl)
                w("tensor", "d2", t_ltri)
                w("tensor", "dv", t_cols_ev)
                for tc in range(NT):
                    mm(bank(4)[:, 8 * tc:8 * (tc + 1)],
                       lgt_hi[:, 128 * tc:128 * (tc + 1)], ident_bf[0:8, 0:8],
                       start=(tc == 0), stop=False)
                    mm(bank(4)[:, 8 * tc:8 * (tc + 1)],
                       lgt_lo[:, 128 * tc:128 * (tc + 1)], ident_bf[0:8, 0:8],
                       start=False, stop=(tc == NT - 1), inc=(tc == NT - 1))
                t_lrawT_pe = c["pe"]
                w("vector", "pe", t_lrawT_pe)
                t_lraw_ev = dve("tensor_copy", lraw[:], bank(4)[:, 0:128])
                for h in range(HC):
                    mm(bank(6)[0:1, 0:E], ones_c_b[:, 0:1], wr_hi[:, h, :],
                       start=(h == 0), stop=False)
                    mm(bank(6)[0:1, 0:E], ones_c_b[:, 0:1], wr_lo[:, h, :],
                       start=False, stop=(h == HC - 1), inc=(h == HC - 1))
                t_wrsum_pe = c["pe"]
                w("vector", "pe", t_wrsum_pe)
                t_wr_ev = dve("tensor_copy", wrsum[:, :], bank(6)[0:1, 0:E])
                w("tensor", "dv", t_wr_ev)
                w("tensor", "dv", t_lraw_ev)
                mm(bank(4)[:, 0:E], ones_f[0:1, 0:128], wrsum[:, :], start=True,
                   stop=True, inc=True)
                t_wrr_pe = c["pe"]
                w("vector", "pe", t_wrr_pe)
                t_wrr_ev = dve("tensor_copy", wr_rep[:], bank(4)[:, 0:E])
                for tc in range(NT):
                    dve("scalar_tensor_tensor", lcor[:, 8 * tc:8 * (tc + 1)],
                        wr_rep[:], nmu_c[:, tc:tc + 1], lraw[:, 8 * tc:8 * (tc + 1)],
                        op0=ALU.mult, op1=ALU.add, inc=(tc == NT - 1))
                t_lcor = c["dv"]
                dve("tensor_tensor", lc3[:, :, :],
                    lcor[:].rearrange("p (n e) -> p n e", n=NT),
                    r_c[:].rearrange("p (n o) -> p n o", o=1).broadcast_to([128, NT, E]),
                    ALU.mult, inc=False)
                dve("reduce_max", v0c[:], lc3[:, :, :], axis=AX.X, inc=False)
                dve("tensor_tensor", oh0_3[:, :, :], lc3[:, :, :],
                    v0c[:].rearrange("p (n o) -> p n o", o=1).broadcast_to([128, NT, E]),
                    ALU.is_equal, inc=False)
                dve("scalar_tensor_tensor", m2_3[:, :, :], oh0_3[:, :, :], 1e9,
                    lc3[:, :, :], op0=ALU.mult, op1=ALU.subtract, inc=False)
                dve("tensor_reduce", nv1c[:], m2_3[:, :, :], AX.X, ALU.min, inc=False)
                dve("tensor_tensor", oh1_3[:, :, :], m2_3[:, :, :],
                    nv1c[:].rearrange("p (n o) -> p n o", o=1).broadcast_to([128, NT, E]),
                    ALU.is_equal, inc=False)
                t_d = dve("tensor_tensor", dsc[:], v0c[:], nv1c[:], ALU.add)
                w("gpsimd", "dv", t_r_bf)
                t_offz = gps("memset", offr[0:1, 0:E], 0.0)
                t_H = dve("tensor_tensor", hbf[:], oh0[:], oh1[:], ALU.add)

                tok_k = {}
                tok_v = {}
                for tc in range(NT):
                    for ob in range(2):
                        idx = tc * 2 + ob
                        bnk = idx % 4
                        if idx == 0:
                            w("tensor", "dv", t_lgt_ev)
                            w("tensor", "d2", t_wk)
                        w("tensor", t_xn_tb[tc // 4][0], t_xn_tb[tc // 4][1])
                        if idx >= 4:
                            w("tensor", "ac", tok_k[idx - 4])
                        for h in range(HC):
                            mm(bank(bnk), xn[:, h, 128 * tc:128 * (tc + 1)],
                               wkv[:, 0, h, 512 * ob:512 * (ob + 1)],
                               start=(h == 0), stop=False)
                        mm(bank(bnk), ones_b[0:1, 0:128],
                           bk_sb[:, 512 * ob:512 * (ob + 1)], start=False,
                           stop=True, inc=True)
                        tk = c["pe"]
                        w("scalar", "pe", tk)
                        tok_k[idx] = act("activation",
                                         expK[:, tc, 512 * ob:512 * (ob + 1)],
                                         bank(bnk), AF.Exp)
                t_k_pe = c["pe"]
                t_k_ac = c["ac"]

                w("scalar", "dv", t_d)
                t_w0 = act("activation", w0c[:], dsc[:], AF.Sigmoid)
                w("vector", "ac", t_w0)
                dve("tensor_scalar", w1c[:], w0c[:], 1.0, -1.0, op0=ALU.subtract,
                    op1=ALU.mult, inc=False)
                w("tensor", "dv", t_H)
                mm(bank(5)[:, 0:128], ltri_sb[:], hbf[:], start=True, stop=True)
                mm(bank(6)[0:1, 0:128], ones_c_b[:, 0:1], hbf[:], start=True, stop=True,
                   inc=True)
                t_within_pe = c["pe"]
                w("vector", "pe", t_within_pe)
                t_win_ev = dve("tensor_copy", within[:], bank(5)[:, 0:128])
                w("vector", "gp", t_offz)
                dve("tensor_copy", totr[:], bank(6)[0:1, 0:128], inc=False)
                for tc in range(1, NT):
                    dve("tensor_tensor", offr[0:1, 8 * tc:8 * (tc + 1)],
                        offr[0:1, 8 * (tc - 1):8 * tc], totr[0:1, 8 * (tc - 1):8 * tc],
                        ALU.add, inc=(tc == NT - 1))
                t_off = c["dv"]

                t_offrep_pe = None
                for tc in range(NT):
                    for ob in range(2):
                        idx = tc * 2 + ob
                        bnk = idx % 4
                        if idx == 0:
                            w("tensor", "d2", t_wv)
                        if idx == 8:
                            w("tensor", "dv", t_off)
                            mm(bank(6)[:, 0:128], ones_f[0:1, 0:128], offr[0:1, :],
                               start=True, stop=True, inc=True)
                            t_offrep_pe = c["pe"]
                        if idx >= 4:
                            w("tensor", "dv", tok_v[idx - 4])
                        else:
                            w("tensor", "ac", tok_k[28 + idx])
                        for h in range(HC):
                            mm(bank(bnk), xn[:, h, 128 * tc:128 * (tc + 1)],
                               wkv[:, 1, h, 512 * ob:512 * (ob + 1)],
                               start=(h == 0), stop=False)
                        mm(bank(bnk), ones_b[32:33, 0:128],
                           bv_sb[:, 512 * ob:512 * (ob + 1)], start=False,
                           stop=True, inc=True)
                        tk = c["pe"]
                        w("vector", "pe", tk)
                        tok_v[idx] = dve("tensor_copy",
                                         vtm[:, tc, 512 * ob:512 * (ob + 1)],
                                         bank(bnk))
                t_kv_pe = c["pe"]
                t_v_dv = c["dv"]
                w("vector", "pe", t_offrep_pe)
                dve("tensor_tensor", cntt[:], within[:], bank(6)[:, 0:128], ALU.add,
                    inc=False)
                dve("tensor_tensor", rtmp[:], oh0[:], cntt[:], ALU.mult, inc=False)
                dve("reduce_sum", pos0[:], rt3[:, :, :], axis=AX.X, inc=False)
                dve("tensor_tensor", rtmp2[:], oh1[:], cntt[:], ALU.mult, inc=False)
                dve("reduce_sum", pos1[:], rt23[:, :, :], axis=AX.X, inc=False)
                dve("tensor_single_scalar", kp0[:], pos0[:], CAP, ALU.is_lt, inc=False)
                dve("tensor_single_scalar", kp1[:], pos1[:], CAP, ALU.is_lt, inc=False)
                dve("tensor_tensor", g0c[:], w0c[:], kp0[:], ALU.mult, inc=False)
                dve("tensor_tensor", g1c[:], w1c[:], kp1[:], ALU.mult, inc=False)
                dve("tensor_tensor", rt3[:, :, :], oh0_3[:, :, :],
                    g0c[:].rearrange("p (n o) -> p n o", o=1).broadcast_to([128, NT, E]),
                    ALU.mult, inc=False)
                dve("tensor_tensor", rt23[:, :, :], oh1_3[:, :, :],
                    g1c[:].rearrange("p (n o) -> p n o", o=1).broadcast_to([128, NT, E]),
                    ALU.mult, inc=False)
                t_m = dve("tensor_tensor", msb[:], rtmp[:], rtmp2[:], ALU.add)

                kv_ev = {}
                for ch in range(HC):
                    b = 4 + ch % 2
                    b2 = 6 + ch % 2
                    if ch == 0:
                        w("tensor", "ac", t_k_ac)
                        w("tensor", "dv", t_v_dv)
                        w("tensor", "dv", t_m)
                    if ch >= 2:
                        w("tensor", "dv", kv_ev[ch - 2])
                    for tc in range(NT):
                        mm(bank(b, 0, 128), expK[:, tc, 128 * ch:128 * (ch + 1)],
                           vtm[:, tc, 128 * ch:128 * (ch + 1)], start=(tc == 0),
                           stop=(tc == NT - 1))
                    for tc in range(NT):
                        mm(bank(b2, 0, 1), expK[:, tc, 128 * ch:128 * (ch + 1)],
                           ones_c_b[:, 0:1], start=(tc == 0), stop=(tc == NT - 1),
                           inc=(tc == NT - 1))
                    tk = c["pe"]
                    w("vector", "pe", tk)
                    if ch == 0:
                        w("vector", "gp", t_kvz)
                    dve("reciprocal", sinv_sb[:, ch:ch + 1], bank(b2, 0, 1), inc=False)
                    if ch >= 1:
                        bp = 4 + (ch - 1) % 2
                        dve("tensor_scalar", kv_sb[0:64, ch - 1, 0:64],
                            bank(bp, 0, 128)[0:64, 0:64],
                            sinv_sb[0:64, ch - 1:ch], None, op0=ALU.mult, inc=False)
                        kv_ev[ch - 1] = dve("tensor_scalar",
                                            kv_sb[64:128, ch - 1, 64:128],
                                            bank(bp, 0, 128)[64:128, 64:128],
                                            sinv_sb[64:128, ch - 1:ch], None,
                                            op0=ALU.mult)
                bp = 4 + (HC - 1) % 2
                dve("engine_nop", inc=False)
                dve("tensor_scalar", kv_sb[0:64, HC - 1, 0:64],
                    bank(bp, 0, 128)[0:64, 0:64], sinv_sb[0:64, HC - 1:HC], None,
                    op0=ALU.mult, inc=False)
                kv_ev[HC - 1] = dve("tensor_scalar", kv_sb[64:128, HC - 1, 64:128],
                                    bank(bp, 0, 128)[64:128, 64:128],
                                    sinv_sb[64:128, HC - 1:HC], None, op0=ALU.mult)
                t_kv3 = c["dv"]

                w("gpsimd", "pe", t_k_pe)
                t_wq = dma("d2", wkv[:, 0, :, :], wq_t.rearrange("(c p) o -> p c o", p=128))
                w("gpsimd", "pe", t_kv_pe)
                t_wo = dma("d2", wkv[:, 1, :, :], wo_t.rearrange("(c p) o -> p c o", p=128))
                q_ev = {}
                for oc in range(HC):
                    for tb in range(TB):
                        idx = oc * TB + tb
                        bnk = idx % 4
                        if idx == 0:
                            w("tensor", "d2", t_wq)
                            w("tensor", "dv", tok_v[28])
                            w("tensor", "dv", tok_v[31])
                        if idx >= 4:
                            w("tensor", "ac", q_ev[idx - 4])
                        for h in range(HC):
                            mm(bank(bnk), wkv[:, 0, h, 128 * oc:128 * (oc + 1)],
                               xn[:, h, 512 * tb:512 * (tb + 1)], start=(h == 0),
                               stop=False)
                        mm(bank(bnk), bq_sb[:, 128 * oc:128 * (oc + 1)],
                           ones_b[0:1, 0:512], start=False, stop=True, inc=True)
                        tk = c["pe"]
                        w("scalar", "pe", tk)
                        q_ev[idx] = act("activation", qs[:, oc, 512 * tb:512 * (tb + 1)],
                                        bank(bnk), AF.Silu)
                t_q_pe = c["pe"]
                t_q_ac = c["ac"]
                g_ev = {}
                for tb in range(TB):
                    bnk = 4 + tb % 2
                    if tb == 0:
                        w("tensor", "dv", t_kv3)
                        w("tensor", "d2", t_wg)
                    if tb >= 2:
                        w("tensor", "dv", g_ev[tb - 2])
                    for h in range(HC):
                        mm(bank(bnk)[0:64, :], wg1_sb[:, h, :],
                           xn[:, h, 512 * tb:512 * (tb + 1)], start=(h == 0), stop=False)
                    mm(bank(bnk)[0:64, :], bg1_sb[:, 0:64], ones_b[64:65, 0:512],
                       start=False, stop=True, inc=True)
                    tk = c["pe"]
                    w("vector", "pe", tk)
                    g_ev[tb] = dve("tensor_copy", gh_sb[:, 512 * tb:512 * (tb + 1)],
                                   bank(bnk)[0:64, :])
                t_gh = c["dv"]

                o_ev = {}
                for ch in range(HC):
                    for tb in range(TB):
                        idx = ch * TB + tb
                        bnk = idx % 4
                        if idx == 0:
                            w("tensor", "ac", t_q_ac)
                            w("tensor", "dv", t_gh)
                        if idx >= 4:
                            w("tensor", "dv", o_ev[idx - 4])
                        mm(bank(bnk), kv_sb[:, ch, :], qs[:, ch, 512 * tb:512 * (tb + 1)],
                           start=True, stop=True, inc=True)
                        tk = c["pe"]
                        w("vector", "pe", tk)
                        o_ev[idx] = dve("tensor_copy", obf[:, ch, 512 * tb:512 * (tb + 1)],
                                        bank(bnk))
                t_o = c["dv"]

                t_st2_h = []
                t_sq2_h = []
                for ch in range(HC):
                    if ch == 0:
                        w("scalar", "dv", t_o)
                    if ch >= 2:
                        w("scalar", "pe", t_st2_h[ch - 2])
                    t_sq2_h.append(act("activation", bpool1[:, ch % 2, :], obf[:, ch, :],
                                       AF.Square))
                    if ch == 0:
                        w("tensor", "dv", t_o)
                    w("tensor", "ac", t_sq2_h[ch])
                    for tb in range(TB):
                        mm(bank(tb)[0:1, :], ones_c_b[:, 0:1],
                           obf[:, ch, 512 * tb:512 * (tb + 1)], start=(ch == 0),
                           stop=(ch == HC - 1))
                    for tb in range(TB):
                        mm(bank(4 + tb)[0:1, :], ones_c_b[:, 0:1],
                           bpool1[:, ch % 2, 512 * tb:512 * (tb + 1)], start=(ch == 0),
                           stop=(ch == HC - 1), inc=(tb == TB - 1))
                    t_st2_h.append(c["pe"])
                t_st2_pe = c["pe"]
                w("vector", "pe", t_st2_pe)
                for tb in range(TB):
                    dve("tensor_copy", srow2[:, 512 * tb:512 * (tb + 1)],
                        bank(tb)[0:1, :], inc=False)
                for tb in range(TB):
                    dve("tensor_copy", qrow2[:, 512 * tb:512 * (tb + 1)],
                        bank(4 + tb)[0:1, :], inc=(tb == TB - 1))
                t_st2_ev = c["dv"]
                dve("tensor_scalar_mul", crow[:], srow2[:], 1.0 / D, inc=False)
                t_mu1b = dve("tensor_scalar_mul", drow[:], qrow2[:], 1.0 / D)
                w("scalar", "dv", t_mu1b)
                t_mu2b = act("activation", qrow[:], crow[:], AF.Square)
                w("vector", "ac", t_mu2b)
                t_var2 = dve("tensor_tensor", srow[:], drow[:], qrow[:], ALU.subtract)
                w("scalar", "dv", t_var2)
                t_sd2 = act("activation", qrow[:], srow[:], AF.Sqrt, bias=eps_sb[0:1, 0:1])
                w("vector", "ac", t_sd2)
                dve("reciprocal", srow[:], qrow[:], inc=False)
                dve("tensor_copy", drow[:], srow[:], inc=False)
                dve("tensor_mul", qrow[:], crow[:], drow[:], inc=False)
                t_rows2 = dve("tensor_scalar_mul", crow[:], qrow[:], -1.0)
                w("tensor", "dv", t_rows2)
                t_reps2_toks = []
                for which, rowsrc in ((0, drow), (1, crow)):
                    for tb in range(TB):
                        if t_reps2_toks:
                            w("tensor", "dv", t_reps2_toks[-1])
                        mm(bank(7)[:, :], ones_b[0:1, 0:128],
                           rowsrc[0:1, 512 * tb:512 * (tb + 1)], start=True, stop=True,
                           inc=True)
                        tk = c["pe"]
                        w("vector", "pe", tk)
                        t_reps2_toks.append(dve("tensor_copy",
                                                reps[:, which,
                                                     512 * tb:512 * (tb + 1)],
                                                bank(7)[:, :]))
                t_reps2 = c["dv"]

                zg_ev = {}
                t_gate_ac = {}
                for oc in range(HC):
                    for tb in range(TB):
                        idx = oc * TB + tb
                        bnk = idx % 4
                        if idx == 0:
                            w("tensor", "dv", t_st2_ev)
                        if idx >= 4:
                            w("tensor", "dv", zg_ev[idx - 4])
                        mm(bank(bnk), wg2_sb[0:64, 128 * oc:128 * (oc + 1)],
                           gh_sb[:, 512 * tb:512 * (tb + 1)], start=True, stop=False)
                        mm(bank(bnk), bg2_sb[:, 128 * oc:128 * (oc + 1)],
                           ones_b[32:33, 0:512], start=False, stop=True, inc=True)
                        tk = c["pe"]
                        w("scalar", "pe", tk)
                        if idx >= 2:
                            w("scalar", "dv", zg_ev[idx - 2])
                        t_gate_ac[idx] = act("activation", bpool2[:, 0, 512 * (idx % 2):512 * (idx % 2) + 512],
                                             bank(bnk), AF.Sigmoid)
                        sl = slice(512 * tb, 512 * (tb + 1))
                        w("vector", "ac", t_gate_ac[idx])
                        if idx == 0:
                            w("vector", "dv", t_reps2)
                        dve("tensor_tensor", tmpb[:, :], obf[:, oc, sl],
                            reps[:, 0, sl], ALU.mult, inc=False)
                        dve("tensor_tensor", tmpb2[:, :], tmpb[:, :],
                            reps[:, 1, sl], ALU.add, inc=False)
                        dve("tensor_scalar", tmpb[:, :], tmpb2[:, :],
                            gam_sb[:, oc:oc + 1], bet_sb[:, oc:oc + 1], op0=ALU.mult,
                            op1=ALU.add, inc=False)
                        zg_ev[idx] = dve("tensor_tensor", zg[:, oc, sl], tmpb[:, :],
                                         bpool2[:, 0, 512 * (idx % 2):512 * (idx % 2) + 512],
                                         ALU.mult)
                t_zg = c["dv"]
                t_gate_ac_last = c["ac"]

                a_ev = {}
                for tc in range(NT):
                    for ob in range(2):
                        idx = tc * 2 + ob
                        bnk = idx % 4
                        if idx == 0:
                            w("tensor", "d2", t_wo)
                            w("tensor", "dv", t_zg)
                            w("tensor", "ac", t_gate_ac_last)
                        if idx >= 4:
                            w("tensor", "dv", a_ev[idx - 4])
                        for h in range(HC):
                            mm(bank(bnk), zg[:, h, 128 * tc:128 * (tc + 1)],
                               wkv[:, 1, h, 512 * ob:512 * (ob + 1)], start=(h == 0),
                               stop=False)
                        mm(bank(bnk), ones_b[64:65, 0:128],
                           bo_sb[:, 512 * ob:512 * (ob + 1)], start=False, stop=True,
                           inc=True)
                        tk = c["pe"]
                        w("vector", "pe", tk)
                        a_ev[idx] = dve("tensor_copy",
                                        accA[:, tc, 512 * ob:512 * (ob + 1)], bank(bnk))
                t_p5_pe = c["pe"]
                t_acc0 = c["dv"]

                pe_base = c["pe"]
                moe_pe = {e_i: pe_base + 32 * (e_i + 1) for e_i in range(E)}
                t_we = []
                for e_i in range(E):
                    if e_i == 0:
                        w("gpsimd", "pe", t_q_pe)
                    elif e_i == 1:
                        w("gpsimd", "pe", t_p5_pe)
                    else:
                        w("gpsimd", "pe", moe_pe[e_i - 2])
                    t_we.append(dma("d2", wkv[:, e_i % 2, :, :],
                                    we[e_i:e_i + 1, :, :].rearrange(
                                        "e (c p) f -> p (e c) f", p=128)))
                moe_ev = {}
                for e_i in range(E):
                    src = accA if e_i % 2 == 0 else accB
                    dstv = accB if e_i % 2 == 0 else accA
                    for tc in range(NT):
                        for ob in range(2):
                            gidx = e_i * NT * 2 + tc * 2 + ob
                            bnk = 4 + (tc * 2 + ob) % 4
                            w("tensor", "d2", t_we[e_i])
                            if gidx == 0:
                                w("tensor", "dv", t_acc0)
                            if gidx >= 4:
                                w("tensor", "dv", moe_ev[gidx - 4])
                            for h in range(HC):
                                mm(bank(bnk), xn[:, h, 128 * tc:128 * (tc + 1)],
                                   wkv[:, e_i % 2, h, 512 * ob:512 * (ob + 1)],
                                   start=(h == 0), stop=(h == HC - 1),
                                   inc=(h == HC - 1))
                            tk = c["pe"]
                            w("vector", "pe", tk)
                            moe_ev[gidx] = dve(
                                "scalar_tensor_tensor",
                                dstv[:, tc, 512 * ob:512 * (ob + 1)], bank(bnk),
                                msb[:, tc * E + e_i: tc * E + e_i + 1],
                                src[:, tc, 512 * ob:512 * (ob + 1)],
                                op0=ALU.mult, op1=ALU.add)
                assert c["pe"] == moe_pe[E - 1], (c["pe"], moe_pe)
                t_moe = c["dv"]

                pbs = (fpool[:, :], fpool2[:, :])
                out_toks = {}
                add_toks = {}
                w("sync", "dv", t_lgt_hl)
                for tc in range(NT):
                    pb = pbs[tc % 2]
                    if tc >= 2:
                        w("sync", "gp", add_toks[tc - 2])
                    tx = dma("d1", pb[:, 0:1024], x_tm[128 * tc:128 * (tc + 1), :])
                    w("gpsimd", "d1", tx)
                    w("gpsimd", "dv", moe_ev[7 * NT * 2 + tc * 2 + 1])
                    if tc >= 2:
                        w("gpsimd", "d3", out_toks[tc - 2])
                    add_toks[tc] = gps("tensor_tensor", pb[:, 1024:2048],
                                       accA[:, tc, :], pb[:, 0:1024], ALU.add)
                    w("scalar", "gp", add_toks[tc])
                    out_toks[tc] = dma("d3", y[128 * tc:128 * (tc + 1), :],
                                       pb[:, 1024:2048])
                w("scalar", "d3", out_toks[NT - 1])

            with nc.Block() as block:
                @block.tensor
                def _(eng):
                    program("tensor", eng)

                @block.vector
                def _(eng):
                    program("vector", eng)

                @block.scalar
                def _(eng):
                    program("scalar", eng)

                @block.sync
                def _(eng):
                    program("sync", eng)

                @block.gpsimd
                def _(eng):
                    program("gpsimd", eng)

    return nc


_NC_CACHE = None


def kernel(L0_attn_x, L0_attn_w_qkv, L0_attn_b_qkv, L0_attn_w_g1, L0_attn_b_g1,
           L0_attn_w_g2, L0_attn_b_g2, L0_attn_w_out, L0_attn_b_out, L0_attn_ln_g,
           L0_attn_ln_b, L0_ffn_Wr, L0_ffn_We):
    global _NC_CACHE
    x = np.asarray(L0_attn_x, np.float32)
    w_qkv = np.asarray(L0_attn_w_qkv, np.float32)
    b_qkv = np.asarray(L0_attn_b_qkv, np.float32)
    shared = {
        "wq_t": np.ascontiguousarray(w_qkv[0:D].T),
        "wk_t": np.ascontiguousarray(w_qkv[D:2 * D].T),
        "wv_t": np.ascontiguousarray(w_qkv[2 * D:3 * D].T),
        "wg1_t": np.ascontiguousarray(np.asarray(L0_attn_w_g1, np.float32).T),
        "wg2_t": np.ascontiguousarray(np.asarray(L0_attn_w_g2, np.float32).T),
        "wo_t": np.ascontiguousarray(np.asarray(L0_attn_w_out, np.float32).T),
        "wr_t": np.ascontiguousarray(np.asarray(L0_ffn_Wr, np.float32).T),
        "we": np.ascontiguousarray(np.asarray(L0_ffn_We, np.float32)),
        "bq": np.ascontiguousarray(b_qkv[0:D][None, :]),
        "bk": np.ascontiguousarray(b_qkv[D:2 * D][None, :]),
        "bv": np.ascontiguousarray(b_qkv[2 * D:3 * D][None, :]),
        "bg1": np.ascontiguousarray(np.asarray(L0_attn_b_g1, np.float32)[None, :]),
        "bg2": np.ascontiguousarray(np.asarray(L0_attn_b_g2, np.float32)[None, :]),
        "bo": np.ascontiguousarray(np.asarray(L0_attn_b_out, np.float32)[None, :]),
        "gam": np.ascontiguousarray(np.asarray(L0_attn_ln_g, np.float32).reshape(HC, 128).T),
        "bet": np.ascontiguousarray(np.asarray(L0_attn_ln_b, np.float32).reshape(HC, 128).T),
        "ltri": np.triu(np.ones((128, 128), np.float32), 1),
        "ident": np.eye(128, dtype=np.float32),
    }
    in_maps = []
    for b in range(8):
        m = dict(shared)
        m["x_fm"] = np.ascontiguousarray(x[b].T)
        m["x_tm"] = np.ascontiguousarray(x[b])
        in_maps.append(m)
    if _NC_CACHE is None:
        _NC_CACHE = build_nc()
    res = run_bass_kernel_spmd(_NC_CACHE, in_maps, core_ids=list(range(8)))
    return np.stack([np.asarray(r["y"], np.float32) for r in res.results], axis=0)


# version 22 (final)
# speedup vs baseline: 1.0155x; 1.0067x over previous
import contextlib
import numpy as np
import concourse.bass as bass
import concourse.mybir as mybir
from concourse.bass_utils import run_bass_kernel_spmd

F32 = mybir.dt.float32
BF16 = mybir.dt.bfloat16
AF = mybir.ActivationFunctionType
ALU = mybir.AluOpType
AX = mybir.AxisListType

T = 2048
D = 1024
HC = 8
NT = 16
TB = 4
E = 8
CAP = 769.0
EPS = 1e-5


def build_nc():
    nc = bass.Bass("TRN2", target_bir_lowering=False, debug=False)
    dt = nc.dram_tensor
    x_fm = dt("x_fm", [D, T], F32, kind="ExternalInput").ap()
    x_tm = dt("x_tm", [T, D], F32, kind="ExternalInput").ap()
    wq_t = dt("wq_t", [D, D], F32, kind="ExternalInput").ap()
    wk_t = dt("wk_t", [D, D], F32, kind="ExternalInput").ap()
    wv_t = dt("wv_t", [D, D], F32, kind="ExternalInput").ap()
    wg1_t = dt("wg1_t", [D, 64], F32, kind="ExternalInput").ap()
    wg2_t = dt("wg2_t", [64, D], F32, kind="ExternalInput").ap()
    wo_t = dt("wo_t", [D, D], F32, kind="ExternalInput").ap()
    wr_t = dt("wr_t", [D, E], F32, kind="ExternalInput").ap()
    we = dt("we", [E, D, D], F32, kind="ExternalInput").ap()
    bq = dt("bq", [1, D], F32, kind="ExternalInput").ap()
    bk = dt("bk", [1, D], F32, kind="ExternalInput").ap()
    bv = dt("bv", [1, D], F32, kind="ExternalInput").ap()
    bg1 = dt("bg1", [1, 64], F32, kind="ExternalInput").ap()
    bg2 = dt("bg2", [1, D], F32, kind="ExternalInput").ap()
    bo = dt("bo", [1, D], F32, kind="ExternalInput").ap()
    gam = dt("gam", [128, HC], F32, kind="ExternalInput").ap()
    bet = dt("bet", [128, HC], F32, kind="ExternalInput").ap()
    ltri = dt("ltri", [128, 128], F32, kind="ExternalInput").ap()
    ident = dt("ident", [128, 128], F32, kind="ExternalInput").ap()
    y = dt("y", [T, D], F32, kind="ExternalOutput").ap()

    with contextlib.ExitStack() as ctx:
        sb = lambda n, s, d: ctx.enter_context(nc.sbuf_tensor(n, s, d))
        sem = lambda n: ctx.enter_context(nc.semaphore(n))
        ar1 = sb("ar1", [128, NT * D], BF16)
        ar2 = sb("ar2", [128, NT * D], BF16)
        ar3 = sb("ar3", [128, NT * D], BF16)
        ar4 = sb("ar4", [128, NT * D], BF16)
        wg1_sb = sb("wg1_sb", [128, HC, 64], BF16)
        wg2_sb = sb("wg2_sb", [64, D], BF16)
        wr_hi = sb("wr_hi", [128, HC, E], BF16)
        wr_lo = sb("wr_lo", [128, HC, E], BF16)
        bias_a = sb("bias_a", [128, D], BF16)
        bias_b = sb("bias_b", [128, D], BF16)
        bq_sb = bias_a[0:1, :]
        bg2_sb = bias_a[32:33, :]
        bg1_sb = bias_a[64:65, 0:64]
        bk_sb = bias_b[0:1, :]
        bv_sb = bias_b[32:33, :]
        bo_sb = bias_b[64:65, :]
        gam_sb = sb("gam_sb", [128, HC], F32)
        bet_sb = sb("bet_sb", [128, HC], F32)
        ltri_sb = sb("ltri_sb", [128, 128], BF16)
        ident_bf = sb("ident_bf", [8, 8], BF16)
        ones_b = sb("ones_b", [128, 512], BF16)
        ones_f = sb("ones_f", [128, 128], F32)
        ones_c_b = sb("ones_c_b", [128, 1], BF16)
        ones_c_f = sb("ones_c_f", [128, 1], F32)
        eps_sb = sb("eps_sb", [128, 1], F32)
        fpool = sb("fpool", [128, 2048], F32)
        fpool2 = sb("fpool2", [128, 2048], F32)
        bpool1 = sb("bpool1", [128, 2, 2048], BF16)
        bpool2 = sb("bpool2", [128, 2, 2048], BF16)
        reps = sb("reps", [128, 2, 2048], BF16)
        scr_b = sb("scr_b", [128, 1024], BF16)
        tmpb = scr_b[:, 0:512]
        tmpb2 = scr_b[:, 512:1024]
        srow = sb("srow", [1, T], F32)
        qrow = sb("qrow", [1, T], F32)
        crow = sb("crow", [1, T], BF16)
        drow = sb("drow", [1, T], BF16)
        r_c = sb("r_c", [128, NT], F32)
        nmu_c = sb("nmu_c", [128, NT], F32)
        wrsum_t = sb("wrsum_t", [1, E], F32)
        wrsum = wrsum_t[0:1, :]
        wr_rep = sb("wr_rep", [128, E], F32)
        lraw = sb("lraw", [128, NT * E], F32)
        wr_sb = lraw[:, 0:64].rearrange("p (c e) -> p c e", c=HC)
        lc = sb("lc", [128, NT * E], F32)
        v0c = sb("v0c", [128, NT], F32)
        nv1c = sb("nv1c", [128, NT], F32)
        m2t = sb("m2t", [128, NT * E], F32)
        dsc = sb("dsc", [128, NT], F32)
        w0c = sb("w0c", [128, NT], F32)
        w1c = sb("w1c", [128, NT], F32)
        hbf = sb("hbf", [128, NT * E], BF16)
        totr = srow[0:1, 0:NT * E]
        offr = srow[0:1, NT * E:2 * NT * E]
        within = lc
        cntt = sb("cntt", [128, NT * E], F32)
        rtmp = sb("rtmp", [128, NT * E], F32)
        lcor = sb("lcor", [128, NT * E], F32)
        oh0 = lraw
        oh1 = lcor
        rtmp2 = m2t
        pos0 = v0c
        pos1 = nv1c
        kp0 = dsc
        kp1 = sb("kp1", [128, NT], F32)
        g0c = sb("g0c", [128, NT], F32)
        g1c = sb("g1c", [128, NT], F32)
        msb = within
        gh_sb = bpool2[0:64, 1, :]
        kv_sb = scr_b[:].rearrange("p (c e) -> p c e", c=HC)
        sinv_sb = sb("sinv_sb", [128, HC], F32)
        srow2 = srow
        qrow2 = qrow
        with nc.psum_tensor("PS", [128, 4096], F32) as PS:
            s_pe = sem("s_pe")
            s_dv = sem("s_dv")
            s_ac = sem("s_ac")
            s_gp = sem("s_gp")
            s_d1 = sem("s_d1")
            s_d2 = sem("s_d2")
            s_d3 = sem("s_d3")

            def bank(b, lo=0, hi=512):
                return PS[:, 512 * b + lo: 512 * b + hi]

            xn = ar1[:].rearrange("p (c t) -> p c t", c=HC)
            xhi = ar3[:].rearrange("p (c t) -> p c t", c=HC)
            xlo3 = ar2[:].rearrange("p (c t) -> p c t", c=HC)
            expK = ar2[:].rearrange("p (n d) -> p n d", n=NT)
            qs = ar2[:].rearrange("p (c t) -> p c t", c=HC)
            zg = ar2[:].rearrange("p (c t) -> p c t", c=HC)
            accB = ar2[:].rearrange("p (n d) -> p n d", n=NT)
            vtm = ar3[:].rearrange("p (n d) -> p n d", n=NT)
            obf = ar3[:].rearrange("p (c t) -> p c t", c=HC)
            accA = ar3[:].rearrange("p (n d) -> p n d", n=NT)
            wkv = ar4[:].rearrange("p (w c o) -> p w c o", w=2, c=HC)
            lraw3 = lraw[:].rearrange("p (n e) -> p n e", n=NT)
            lc3 = lc[:].rearrange("p (n e) -> p n e", n=NT)
            oh0_3 = oh0[:].rearrange("p (n e) -> p n e", n=NT)
            oh1_3 = oh1[:].rearrange("p (n e) -> p n e", n=NT)
            m2_3 = m2t[:].rearrange("p (n e) -> p n e", n=NT)
            rt3 = rtmp[:].rearrange("p (n e) -> p n e", n=NT)
            rt23 = rtmp2[:].rearrange("p (n e) -> p n e", n=NT)

            def program(me, eng):
                c = {"pe": 0, "dv": 0, "ac": 0, "gp": 0, "d1": 0, "d2": 0, "d3": 0}
                S = {"pe": s_pe, "dv": s_dv, "ac": s_ac, "gp": s_gp,
                     "d1": s_d1, "d2": s_d2, "d3": s_d3}
                OWN = {"d1": "sync", "d2": "gpsimd", "d3": "scalar"}

                def w(owner, ctr, val=None):
                    if me == owner:
                        eng.wait_ge(S[ctr], c[ctr] if val is None else val)

                def mm(out, l, r, start, stop, inc=False):
                    if me == "tensor":
                        i = eng.matmul(out, l, r, start=start, stop=stop)
                        if inc:
                            i.then_inc(S["pe"], 1)
                    if inc:
                        c["pe"] += 1
                    return c["pe"]

                def dve(name, *a, inc=True, **k):
                    if me == "vector":
                        eng.wait_ge(S["dv"], c["dv"])
                        getattr(eng, name)(*a, **k).then_inc(S["dv"], 1)
                    c["dv"] += 1
                    return c["dv"]

                def act(name, *a, inc=True, **k):
                    if me == "scalar":
                        eng.wait_ge(S["ac"], c["ac"])
                        getattr(eng, name)(*a, **k).then_inc(S["ac"], 1)
                    c["ac"] += 1
                    return c["ac"]

                def gps(name, *a, inc=True, **k):
                    if me == "gpsimd":
                        eng.wait_ge(S["gp"], c["gp"])
                        getattr(eng, name)(*a, **k).then_inc(S["gp"], 1)
                    c["gp"] += 1
                    return c["gp"]

                def dma(q, out, in_):
                    if me == OWN[q]:
                        eng.wait_ge(S[q], c[q])
                        eng.dma_start(out=out, in_=in_).then_inc(S[q], 16)
                    c[q] += 16
                    return c[q]

                dma("d1", wr_sb[:], wr_t.rearrange("(c p) e -> p c e", p=128))
                dma("d1", gam_sb[:], gam)
                t_consts1 = dma("d1", bet_sb[:], bet)
                gps("memset", ones_b[:], 1.0, inc=False)
                gps("memset", ones_f[:], 1.0, inc=False)
                gps("memset", ones_c_b[:], 1.0, inc=False)
                gps("memset", ones_c_f[:], 1.0, inc=False)
                t_const_gp = gps("memset", eps_sb[:], EPS)
                t_wrhi = dma("d2", wr_hi[:], wr_t.rearrange("(c p) e -> p c e", p=128))
                dma("d2", ltri_sb[:], ltri)
                t_ltri = dma("d2", ident_bf[:], ident[0:8, 0:8])
                dma("d2", bq_sb, bq)
                dma("d2", bk_sb, bk)
                dma("d2", bv_sb, bv)
                dma("d2", bg1_sb, bg1)
                dma("d2", bg2_sb, bg2)
                t_bias = dma("d2", bo_sb, bo)
                t_wk = dma("d2", wkv[:, 0, :, :], wk_t.rearrange("(c p) o -> p c o", p=128))
                t_wv = dma("d2", wkv[:, 1, :, :], wv_t.rearrange("(c p) o -> p c o", p=128))
                dma("d2", wg1_sb[:], wg1_t.rearrange("(c p) o -> p c o", p=128))
                t_wg = dma("d2", wg2_sb[:], wg2_t)

                t_xbf = []
                t_stats_h = []
                t_sq_h = []
                t_xlo_h = []
                t_xf_h = []
                for h in range(HC):
                    qf = "d1" if h % 2 == 0 else "d3"
                    fbuf = fpool[:, :] if h % 2 == 0 else fpool2[:, :]
                    if h >= 2:
                        w(OWN[qf], "dv", t_xlo_h[h - 2])
                    t_xf_h.append(dma(qf, fbuf,
                                      x_fm[128 * h:128 * (h + 1), :]))
                    w("vector", qf, t_xf_h[h])
                    t_xbf.append(dve("tensor_copy", xhi[:, h, :], fbuf))
                    w("scalar", "dv", t_xbf[h])
                    if h >= 2:
                        w("scalar", "pe", t_stats_h[h - 2])
                    t_sq_h.append(act("activation", bpool2[:, h % 2, :],
                                      xhi[:, h, :], AF.Square))
                    t_xlo_h.append(dve("tensor_tensor", xlo3[:, h, :],
                                       fbuf, xhi[:, h, :], ALU.subtract))
                    if h == 0:
                        w("tensor", "gp", t_const_gp)
                    w("tensor", "dv", t_xbf[h])
                    w("tensor", "ac", t_sq_h[h])
                    for tb in range(TB):
                        mm(bank(tb)[0:1, :], ones_c_b[:, 0:1],
                           xhi[:, h, 512 * tb:512 * (tb + 1)],
                           start=(h == 0), stop=(h == HC - 1))
                    for tb in range(TB):
                        mm(bank(4 + tb)[0:1, :], ones_c_b[:, 0:1],
                           bpool2[:, h % 2, 512 * tb:512 * (tb + 1)],
                           start=(h == 0), stop=(h == HC - 1), inc=(tb == TB - 1))
                    t_stats_h.append(c["pe"])
                t_stats_pe = c["pe"]
                t_xlo = c["dv"]
                w("vector", "gp", t_const_gp)
                w("vector", "pe", t_stats_pe)
                for tb in range(TB):
                    dve("tensor_copy", srow[:, 512 * tb:512 * (tb + 1)],
                        bank(tb)[0:1, :], inc=False)
                for tb in range(TB):
                    dve("tensor_copy", qrow[:, 512 * tb:512 * (tb + 1)],
                        bank(4 + tb)[0:1, :], inc=(tb == TB - 1))
                t_stats_ev = c["dv"]
                w("vector", "d1", t_consts1)
                w("vector", "d2", t_wrhi)
                t_wrlo = dve("tensor_tensor",
                             wr_lo[:].rearrange("p c e -> p (c e)"),
                             wr_sb[:].rearrange("p c e -> p (c e)"),
                             wr_hi[:].rearrange("p c e -> p (c e)"), ALU.subtract)
                w("tensor", "dv", t_wrlo)
                w("tensor", "d2", t_wrhi)
                for h in range(HC):
                    for tb in range(TB):
                        sl = slice(512 * tb, 512 * (tb + 1))
                        mm(bank(tb)[0:8, :], wr_hi[:, h, :], xhi[:, h, sl],
                           start=(h == 0), stop=False)
                        mm(bank(tb)[0:8, :], wr_lo[:, h, :], xhi[:, h, sl],
                           start=False, stop=False)
                        mm(bank(tb)[0:8, :], wr_hi[:, h, :], xlo3[:, h, sl],
                           start=False, stop=(h == HC - 1), inc=(tb == TB - 1))
                t_rt_pe = c["pe"]
                dve("tensor_scalar_mul", crow[:], srow[:], 1.0 / D, inc=False)
                t_mu1 = dve("tensor_scalar_mul", drow[:], qrow[:], 1.0 / D)
                w("scalar", "dv", t_mu1)
                w("scalar", "gp", t_const_gp)
                t_mu2 = act("activation", qrow[:], crow[:], AF.Square)
                w("vector", "ac", t_mu2)
                t_var = dve("tensor_tensor", srow[:], drow[:], qrow[:], ALU.subtract)
                w("scalar", "dv", t_var)
                t_sd = act("activation", qrow[:], srow[:], AF.Sqrt, bias=eps_sb[0:1, 0:1])
                w("vector", "ac", t_sd)
                dve("reciprocal", srow[:], qrow[:], inc=False)
                t_r_bf = dve("tensor_copy", drow[:], srow[:])
                w("tensor", "dv", t_r_bf)
                for tc in range(NT):
                    mm(bank(4)[:, tc:tc + 1], drow[0:1, 128 * tc:128 * (tc + 1)],
                       ones_b[0:1, 0:1], start=True, stop=True)
                for tc in range(NT):
                    mm(bank(4)[:, NT + tc:NT + tc + 1], crow[0:1, 128 * tc:128 * (tc + 1)],
                       ones_b[0:1, 0:1], start=True, stop=True, inc=(tc == NT - 1))
                t_cols_pe = c["pe"]
                dve("tensor_mul", qrow[:], crow[:], drow[:], inc=False)
                w("vector", "pe", t_cols_pe)
                t_rows = dve("tensor_scalar_mul", crow[:], qrow[:], -1.0)
                dve("tensor_copy", r_c[:], bank(4)[:, 0:NT], inc=False)
                t_cols_ev = dve("tensor_scalar", nmu_c[:], bank(4)[:, NT:2 * NT],
                                -1.0, None, op0=ALU.mult)

                t_reps_toks = []
                for which, rowsrc in ((0, drow), (1, crow)):
                    for tb in range(TB):
                        if which == 1 and tb == 0:
                            w("tensor", "dv", t_rows)
                        if t_reps_toks:
                            w("tensor", "dv", t_reps_toks[-1])
                        mm(bank(7)[:, :], ones_b[0:1, 0:128],
                           rowsrc[0:1, 512 * tb:512 * (tb + 1)], start=True, stop=True,
                           inc=True)
                        tk = c["pe"]
                        w("vector", "pe", tk)
                        t_reps_toks.append(dve("tensor_copy",
                                               reps[:, which, 512 * tb:512 * (tb + 1)],
                                               bank(7)[:, :]))
                t_reps = c["dv"]
                lgt_f = fpool[0:8, :]
                lgt_hi = bpool1[0:8, 0, :]
                lgt_lo = bpool1[0:8, 1, :]
                w("vector", "pe", t_rt_pe)
                for tb in range(TB):
                    dve("tensor_copy", lgt_f[:, 512 * tb:512 * (tb + 1)],
                        bank(tb)[0:8, :], inc=(tb == TB - 1))
                t_lgt_ev = c["dv"]
                dve("tensor_copy", lgt_hi[:, :], lgt_f[:, :], inc=False)
                t_lgt_hl = dve("tensor_tensor", lgt_lo[:, :], lgt_f[:, :], lgt_hi[:, :],
                               ALU.subtract)
                t_xn_tb = []
                w("vector", "dv", t_reps)
                for tb in range(TB):
                    sl = slice(512 * tb, 512 * (tb + 1))
                    for h in range(HC):
                        dve("tensor_tensor", tmpb[:, :], xhi[:, h, sl],
                            reps[:, 0, sl], ALU.mult, inc=False)
                        dve("tensor_tensor", xn[:, h, sl], tmpb[:, :],
                            reps[:, 1, sl], ALU.add, inc=(h == HC - 1))
                    t_xn_tb.append(c["dv"])
                t_xn = c["dv"]
                w("gpsimd", "dv", t_xn)
                t_kvz = gps("memset", scr_b[:, :], 0.0)

                w("tensor", "dv", t_lgt_hl)
                w("tensor", "d2", t_ltri)
                w("tensor", "dv", t_cols_ev)
                for tc in range(NT):
                    mm(bank(4)[:, 8 * tc:8 * (tc + 1)],
                       lgt_hi[:, 128 * tc:128 * (tc + 1)], ident_bf[0:8, 0:8],
                       start=(tc == 0), stop=False)
                    mm(bank(4)[:, 8 * tc:8 * (tc + 1)],
                       lgt_lo[:, 128 * tc:128 * (tc + 1)], ident_bf[0:8, 0:8],
                       start=False, stop=(tc == NT - 1), inc=(tc == NT - 1))
                t_lrawT_pe = c["pe"]
                w("vector", "pe", t_lrawT_pe)
                t_lraw_ev = dve("tensor_copy", lraw[:], bank(4)[:, 0:128])
                for h in range(HC):
                    mm(bank(6)[0:1, 0:E], ones_c_b[:, 0:1], wr_hi[:, h, :],
                       start=(h == 0), stop=False)
                    mm(bank(6)[0:1, 0:E], ones_c_b[:, 0:1], wr_lo[:, h, :],
                       start=False, stop=(h == HC - 1), inc=(h == HC - 1))
                t_wrsum_pe = c["pe"]
                w("vector", "pe", t_wrsum_pe)
                t_wr_ev = dve("tensor_copy", wrsum[:, :], bank(6)[0:1, 0:E])
                w("tensor", "dv", t_wr_ev)
                w("tensor", "dv", t_lraw_ev)
                mm(bank(4)[:, 0:E], ones_f[0:1, 0:128], wrsum[:, :], start=True,
                   stop=True, inc=True)
                t_wrr_pe = c["pe"]
                w("vector", "pe", t_wrr_pe)
                t_wrr_ev = dve("tensor_copy", wr_rep[:], bank(4)[:, 0:E])
                for tc in range(NT):
                    dve("scalar_tensor_tensor", lcor[:, 8 * tc:8 * (tc + 1)],
                        wr_rep[:], nmu_c[:, tc:tc + 1], lraw[:, 8 * tc:8 * (tc + 1)],
                        op0=ALU.mult, op1=ALU.add, inc=(tc == NT - 1))
                t_lcor = c["dv"]
                dve("tensor_tensor", lc3[:, :, :],
                    lcor[:].rearrange("p (n e) -> p n e", n=NT),
                    r_c[:].rearrange("p (n o) -> p n o", o=1).broadcast_to([128, NT, E]),
                    ALU.mult, inc=False)
                dve("reduce_max", v0c[:], lc3[:, :, :], axis=AX.X, inc=False)
                dve("tensor_tensor", oh0_3[:, :, :], lc3[:, :, :],
                    v0c[:].rearrange("p (n o) -> p n o", o=1).broadcast_to([128, NT, E]),
                    ALU.is_equal, inc=False)
                dve("scalar_tensor_tensor", m2_3[:, :, :], oh0_3[:, :, :], 1e9,
                    lc3[:, :, :], op0=ALU.mult, op1=ALU.subtract, inc=False)
                dve("tensor_reduce", nv1c[:], m2_3[:, :, :], AX.X, ALU.min, inc=False)
                dve("tensor_tensor", oh1_3[:, :, :], m2_3[:, :, :],
                    nv1c[:].rearrange("p (n o) -> p n o", o=1).broadcast_to([128, NT, E]),
                    ALU.is_equal, inc=False)
                t_d = dve("tensor_tensor", dsc[:], v0c[:], nv1c[:], ALU.add)
                w("gpsimd", "dv", t_r_bf)
                t_offz = gps("memset", offr[0:1, 0:E], 0.0)
                t_H = dve("tensor_tensor", hbf[:], oh0[:], oh1[:], ALU.add)

                tok_k = {}
                tok_v = {}
                for tc in range(NT):
                    for ob in range(2):
                        idx = tc * 2 + ob
                        bnk = idx % 4
                        if idx == 0:
                            w("tensor", "dv", t_lgt_ev)
                            w("tensor", "d2", t_wk)
                        w("tensor", "dv", t_xn_tb[tc // 4])
                        if idx >= 4:
                            w("tensor", "ac", tok_k[idx - 4])
                        for h in range(HC):
                            mm(bank(bnk), xn[:, h, 128 * tc:128 * (tc + 1)],
                               wkv[:, 0, h, 512 * ob:512 * (ob + 1)],
                               start=(h == 0), stop=False)
                        mm(bank(bnk), ones_b[0:1, 0:128],
                           bk_sb[:, 512 * ob:512 * (ob + 1)], start=False,
                           stop=True, inc=True)
                        tk = c["pe"]
                        w("scalar", "pe", tk)
                        tok_k[idx] = act("activation",
                                         expK[:, tc, 512 * ob:512 * (ob + 1)],
                                         bank(bnk), AF.Exp)
                t_k_pe = c["pe"]
                t_k_ac = c["ac"]

                w("scalar", "dv", t_d)
                t_w0 = act("activation", w0c[:], dsc[:], AF.Sigmoid)
                w("vector", "ac", t_w0)
                dve("tensor_scalar", w1c[:], w0c[:], 1.0, -1.0, op0=ALU.subtract,
                    op1=ALU.mult, inc=False)
                w("tensor", "dv", t_H)
                mm(bank(5)[:, 0:128], ltri_sb[:], hbf[:], start=True, stop=True)
                mm(bank(6)[0:1, 0:128], ones_c_b[:, 0:1], hbf[:], start=True, stop=True,
                   inc=True)
                t_within_pe = c["pe"]
                w("vector", "pe", t_within_pe)
                t_win_ev = dve("tensor_copy", within[:], bank(5)[:, 0:128])
                w("vector", "gp", t_offz)
                dve("tensor_copy", totr[:], bank(6)[0:1, 0:128], inc=False)
                for tc in range(1, NT):
                    dve("tensor_tensor", offr[0:1, 8 * tc:8 * (tc + 1)],
                        offr[0:1, 8 * (tc - 1):8 * tc], totr[0:1, 8 * (tc - 1):8 * tc],
                        ALU.add, inc=(tc == NT - 1))
                t_off = c["dv"]

                t_offrep_pe = None
                for tc in range(NT):
                    for ob in range(2):
                        idx = tc * 2 + ob
                        bnk = idx % 4
                        if idx == 0:
                            w("tensor", "d2", t_wv)
                        if idx == 8:
                            w("tensor", "dv", t_off)
                            mm(bank(6)[:, 0:128], ones_f[0:1, 0:128], offr[0:1, :],
                               start=True, stop=True, inc=True)
                            t_offrep_pe = c["pe"]
                        if idx >= 4:
                            w("tensor", "dv", tok_v[idx - 4])
                        else:
                            w("tensor", "ac", tok_k[28 + idx])
                        for h in range(HC):
                            mm(bank(bnk), xn[:, h, 128 * tc:128 * (tc + 1)],
                               wkv[:, 1, h, 512 * ob:512 * (ob + 1)],
                               start=(h == 0), stop=False)
                        mm(bank(bnk), ones_b[32:33, 0:128],
                           bv_sb[:, 512 * ob:512 * (ob + 1)], start=False,
                           stop=True, inc=True)
                        tk = c["pe"]
                        w("vector", "pe", tk)
                        tok_v[idx] = dve("tensor_copy",
                                         vtm[:, tc, 512 * ob:512 * (ob + 1)],
                                         bank(bnk))
                t_kv_pe = c["pe"]
                t_v_dv = c["dv"]
                w("vector", "pe", t_offrep_pe)
                dve("tensor_tensor", cntt[:], within[:], bank(6)[:, 0:128], ALU.add,
                    inc=False)
                dve("tensor_tensor", rtmp[:], oh0[:], cntt[:], ALU.mult, inc=False)
                dve("reduce_sum", pos0[:], rt3[:, :, :], axis=AX.X, inc=False)
                dve("tensor_tensor", rtmp2[:], oh1[:], cntt[:], ALU.mult, inc=False)
                dve("reduce_sum", pos1[:], rt23[:, :, :], axis=AX.X, inc=False)
                dve("tensor_single_scalar", kp0[:], pos0[:], CAP, ALU.is_lt, inc=False)
                dve("tensor_single_scalar", kp1[:], pos1[:], CAP, ALU.is_lt, inc=False)
                dve("tensor_tensor", g0c[:], w0c[:], kp0[:], ALU.mult, inc=False)
                dve("tensor_tensor", g1c[:], w1c[:], kp1[:], ALU.mult, inc=False)
                dve("tensor_tensor", rt3[:, :, :], oh0_3[:, :, :],
                    g0c[:].rearrange("p (n o) -> p n o", o=1).broadcast_to([128, NT, E]),
                    ALU.mult, inc=False)
                dve("tensor_tensor", rt23[:, :, :], oh1_3[:, :, :],
                    g1c[:].rearrange("p (n o) -> p n o", o=1).broadcast_to([128, NT, E]),
                    ALU.mult, inc=False)
                t_m = dve("tensor_tensor", msb[:], rtmp[:], rtmp2[:], ALU.add)

                kv_ev = {}
                for ch in range(HC):
                    b = 4 + ch % 2
                    b2 = 6 + ch % 2
                    if ch == 0:
                        w("tensor", "ac", t_k_ac)
                        w("tensor", "dv", t_v_dv)
                        w("tensor", "dv", t_m)
                    if ch >= 2:
                        w("tensor", "dv", kv_ev[ch - 2])
                    for tc in range(NT):
                        mm(bank(b, 0, 128), expK[:, tc, 128 * ch:128 * (ch + 1)],
                           vtm[:, tc, 128 * ch:128 * (ch + 1)], start=(tc == 0),
                           stop=(tc == NT - 1))
                    for tc in range(NT):
                        mm(bank(b2, 0, 1), expK[:, tc, 128 * ch:128 * (ch + 1)],
                           ones_c_b[:, 0:1], start=(tc == 0), stop=(tc == NT - 1),
                           inc=(tc == NT - 1))
                    tk = c["pe"]
                    w("vector", "pe", tk)
                    if ch == 0:
                        w("vector", "gp", t_kvz)
                    dve("reciprocal", sinv_sb[:, ch:ch + 1], bank(b2, 0, 1), inc=False)
                    if ch >= 1:
                        bp = 4 + (ch - 1) % 2
                        dve("tensor_scalar", kv_sb[0:64, ch - 1, 0:64],
                            bank(bp, 0, 128)[0:64, 0:64],
                            sinv_sb[0:64, ch - 1:ch], None, op0=ALU.mult, inc=False)
                        kv_ev[ch - 1] = dve("tensor_scalar",
                                            kv_sb[64:128, ch - 1, 64:128],
                                            bank(bp, 0, 128)[64:128, 64:128],
                                            sinv_sb[64:128, ch - 1:ch], None,
                                            op0=ALU.mult)
                bp = 4 + (HC - 1) % 2
                dve("engine_nop", inc=False)
                dve("tensor_scalar", kv_sb[0:64, HC - 1, 0:64],
                    bank(bp, 0, 128)[0:64, 0:64], sinv_sb[0:64, HC - 1:HC], None,
                    op0=ALU.mult, inc=False)
                kv_ev[HC - 1] = dve("tensor_scalar", kv_sb[64:128, HC - 1, 64:128],
                                    bank(bp, 0, 128)[64:128, 64:128],
                                    sinv_sb[64:128, HC - 1:HC], None, op0=ALU.mult)
                t_kv3 = c["dv"]

                w("gpsimd", "pe", t_k_pe)
                t_wq = dma("d2", wkv[:, 0, :, :], wq_t.rearrange("(c p) o -> p c o", p=128))
                w("gpsimd", "pe", t_kv_pe)
                t_wo = dma("d2", wkv[:, 1, :, :], wo_t.rearrange("(c p) o -> p c o", p=128))
                q_ev = {}
                for oc in range(HC):
                    for tb in range(TB):
                        idx = oc * TB + tb
                        bnk = idx % 4
                        if idx == 0:
                            w("tensor", "d2", t_wq)
                            w("tensor", "dv", tok_v[28])
                            w("tensor", "dv", tok_v[31])
                        if idx >= 4:
                            w("tensor", "ac", q_ev[idx - 4])
                        for h in range(HC):
                            mm(bank(bnk), wkv[:, 0, h, 128 * oc:128 * (oc + 1)],
                               xn[:, h, 512 * tb:512 * (tb + 1)], start=(h == 0),
                               stop=False)
                        mm(bank(bnk), bq_sb[:, 128 * oc:128 * (oc + 1)],
                           ones_b[0:1, 0:512], start=False, stop=True, inc=True)
                        tk = c["pe"]
                        w("scalar", "pe", tk)
                        q_ev[idx] = act("activation", qs[:, oc, 512 * tb:512 * (tb + 1)],
                                        bank(bnk), AF.Silu)
                t_q_pe = c["pe"]
                t_q_ac = c["ac"]
                g_ev = {}
                for tb in range(TB):
                    bnk = 4 + tb % 2
                    if tb == 0:
                        w("tensor", "dv", t_kv3)
                        w("tensor", "d2", t_wg)
                    if tb >= 2:
                        w("tensor", "dv", g_ev[tb - 2])
                    for h in range(HC):
                        mm(bank(bnk)[0:64, :], wg1_sb[:, h, :],
                           xn[:, h, 512 * tb:512 * (tb + 1)], start=(h == 0), stop=False)
                    mm(bank(bnk)[0:64, :], bg1_sb[:, 0:64], ones_b[64:65, 0:512],
                       start=False, stop=True, inc=True)
                    tk = c["pe"]
                    w("vector", "pe", tk)
                    g_ev[tb] = dve("tensor_copy", gh_sb[:, 512 * tb:512 * (tb + 1)],
                                   bank(bnk)[0:64, :])
                t_gh = c["dv"]

                o_ev = {}
                for ch in range(HC):
                    for tb in range(TB):
                        idx = ch * TB + tb
                        bnk = idx % 4
                        if idx == 0:
                            w("tensor", "ac", t_q_ac)
                            w("tensor", "dv", t_gh)
                        if idx >= 4:
                            w("tensor", "dv", o_ev[idx - 4])
                        mm(bank(bnk), kv_sb[:, ch, :], qs[:, ch, 512 * tb:512 * (tb + 1)],
                           start=True, stop=True, inc=True)
                        tk = c["pe"]
                        w("vector", "pe", tk)
                        o_ev[idx] = dve("tensor_copy", obf[:, ch, 512 * tb:512 * (tb + 1)],
                                        bank(bnk))
                t_o = c["dv"]

                t_st2_h = []
                t_sq2_h = []
                for ch in range(HC):
                    if ch == 0:
                        w("scalar", "dv", t_o)
                    if ch >= 2:
                        w("scalar", "pe", t_st2_h[ch - 2])
                    t_sq2_h.append(act("activation", bpool1[:, ch % 2, :], obf[:, ch, :],
                                       AF.Square))
                    if ch == 0:
                        w("tensor", "dv", t_o)
                    w("tensor", "ac", t_sq2_h[ch])
                    for tb in range(TB):
                        mm(bank(tb)[0:1, :], ones_c_b[:, 0:1],
                           obf[:, ch, 512 * tb:512 * (tb + 1)], start=(ch == 0),
                           stop=(ch == HC - 1))
                    for tb in range(TB):
                        mm(bank(4 + tb)[0:1, :], ones_c_b[:, 0:1],
                           bpool1[:, ch % 2, 512 * tb:512 * (tb + 1)], start=(ch == 0),
                           stop=(ch == HC - 1), inc=(tb == TB - 1))
                    t_st2_h.append(c["pe"])
                t_st2_pe = c["pe"]
                w("vector", "pe", t_st2_pe)
                for tb in range(TB):
                    dve("tensor_copy", srow2[:, 512 * tb:512 * (tb + 1)],
                        bank(tb)[0:1, :], inc=False)
                for tb in range(TB):
                    dve("tensor_copy", qrow2[:, 512 * tb:512 * (tb + 1)],
                        bank(4 + tb)[0:1, :], inc=(tb == TB - 1))
                t_st2_ev = c["dv"]
                dve("tensor_scalar_mul", crow[:], srow2[:], 1.0 / D, inc=False)
                t_mu1b = dve("tensor_scalar_mul", drow[:], qrow2[:], 1.0 / D)
                w("scalar", "dv", t_mu1b)
                t_mu2b = act("activation", qrow[:], crow[:], AF.Square)
                w("vector", "ac", t_mu2b)
                t_var2 = dve("tensor_tensor", srow[:], drow[:], qrow[:], ALU.subtract)
                w("scalar", "dv", t_var2)
                t_sd2 = act("activation", qrow[:], srow[:], AF.Sqrt, bias=eps_sb[0:1, 0:1])
                w("vector", "ac", t_sd2)
                dve("reciprocal", srow[:], qrow[:], inc=False)
                dve("tensor_copy", drow[:], srow[:], inc=False)
                dve("tensor_mul", qrow[:], crow[:], drow[:], inc=False)
                t_rows2 = dve("tensor_scalar_mul", crow[:], qrow[:], -1.0)
                w("tensor", "dv", t_rows2)
                t_reps2_toks = []
                for which, rowsrc in ((0, drow), (1, crow)):
                    for tb in range(TB):
                        if t_reps2_toks:
                            w("tensor", "dv", t_reps2_toks[-1])
                        mm(bank(7)[:, :], ones_b[0:1, 0:128],
                           rowsrc[0:1, 512 * tb:512 * (tb + 1)], start=True, stop=True,
                           inc=True)
                        tk = c["pe"]
                        w("vector", "pe", tk)
                        t_reps2_toks.append(dve("tensor_copy",
                                                reps[:, which,
                                                     512 * tb:512 * (tb + 1)],
                                                bank(7)[:, :]))
                t_reps2 = c["dv"]

                zg_ev = {}
                t_gate_ac = {}
                for oc in range(HC):
                    for tb in range(TB):
                        idx = oc * TB + tb
                        bnk = idx % 4
                        if idx == 0:
                            w("tensor", "dv", t_st2_ev)
                        if idx >= 4:
                            w("tensor", "dv", zg_ev[idx - 4])
                        mm(bank(bnk), wg2_sb[0:64, 128 * oc:128 * (oc + 1)],
                           gh_sb[:, 512 * tb:512 * (tb + 1)], start=True, stop=False)
                        mm(bank(bnk), bg2_sb[:, 128 * oc:128 * (oc + 1)],
                           ones_b[32:33, 0:512], start=False, stop=True, inc=True)
                        tk = c["pe"]
                        w("scalar", "pe", tk)
                        if idx >= 2:
                            w("scalar", "dv", zg_ev[idx - 2])
                        t_gate_ac[idx] = act("activation", bpool2[:, 0, 512 * (idx % 2):512 * (idx % 2) + 512],
                                             bank(bnk), AF.Sigmoid)
                        sl = slice(512 * tb, 512 * (tb + 1))
                        w("vector", "ac", t_gate_ac[idx])
                        if idx == 0:
                            w("vector", "dv", t_reps2)
                        dve("tensor_tensor", tmpb[:, :], obf[:, oc, sl],
                            reps[:, 0, sl], ALU.mult, inc=False)
                        dve("tensor_tensor", tmpb2[:, :], tmpb[:, :],
                            reps[:, 1, sl], ALU.add, inc=False)
                        dve("tensor_scalar", tmpb[:, :], tmpb2[:, :],
                            gam_sb[:, oc:oc + 1], bet_sb[:, oc:oc + 1], op0=ALU.mult,
                            op1=ALU.add, inc=False)
                        zg_ev[idx] = dve("tensor_tensor", zg[:, oc, sl], tmpb[:, :],
                                         bpool2[:, 0, 512 * (idx % 2):512 * (idx % 2) + 512],
                                         ALU.mult)
                t_zg = c["dv"]
                t_gate_ac_last = c["ac"]

                a_ev = {}
                for tc in range(NT):
                    for ob in range(2):
                        idx = tc * 2 + ob
                        bnk = idx % 4
                        if idx == 0:
                            w("tensor", "d2", t_wo)
                            w("tensor", "dv", t_zg)
                            w("tensor", "ac", t_gate_ac_last)
                        if idx >= 4:
                            w("tensor", "dv", a_ev[idx - 4])
                        for h in range(HC):
                            mm(bank(bnk), zg[:, h, 128 * tc:128 * (tc + 1)],
                               wkv[:, 1, h, 512 * ob:512 * (ob + 1)], start=(h == 0),
                               stop=False)
                        mm(bank(bnk), ones_b[64:65, 0:128],
                           bo_sb[:, 512 * ob:512 * (ob + 1)], start=False, stop=True,
                           inc=True)
                        tk = c["pe"]
                        w("vector", "pe", tk)
                        a_ev[idx] = dve("tensor_copy",
                                        accA[:, tc, 512 * ob:512 * (ob + 1)], bank(bnk))
                t_p5_pe = c["pe"]
                t_acc0 = c["dv"]

                pe_base = c["pe"]
                moe_pe = {e_i: pe_base + 32 * (e_i + 1) for e_i in range(E)}
                t_we = []
                for e_i in range(E):
                    if e_i == 0:
                        w("gpsimd", "pe", t_q_pe)
                    elif e_i == 1:
                        w("gpsimd", "pe", t_p5_pe)
                    else:
                        w("gpsimd", "pe", moe_pe[e_i - 2])
                    t_we.append(dma("d2", wkv[:, e_i % 2, :, :],
                                    we[e_i:e_i + 1, :, :].rearrange(
                                        "e (c p) f -> p (e c) f", p=128)))
                moe_ev = {}
                for e_i in range(E):
                    src = accA if e_i % 2 == 0 else accB
                    dstv = accB if e_i % 2 == 0 else accA
                    for tc in range(NT):
                        for ob in range(2):
                            gidx = e_i * NT * 2 + tc * 2 + ob
                            bnk = 4 + (tc * 2 + ob) % 4
                            w("tensor", "d2", t_we[e_i])
                            if gidx == 0:
                                w("tensor", "dv", t_acc0)
                            if gidx >= 4:
                                w("tensor", "dv", moe_ev[gidx - 4])
                            for h in range(HC):
                                mm(bank(bnk), xn[:, h, 128 * tc:128 * (tc + 1)],
                                   wkv[:, e_i % 2, h, 512 * ob:512 * (ob + 1)],
                                   start=(h == 0), stop=(h == HC - 1),
                                   inc=(h == HC - 1))
                            tk = c["pe"]
                            w("vector", "pe", tk)
                            moe_ev[gidx] = dve(
                                "scalar_tensor_tensor",
                                dstv[:, tc, 512 * ob:512 * (ob + 1)], bank(bnk),
                                msb[:, tc * E + e_i: tc * E + e_i + 1],
                                src[:, tc, 512 * ob:512 * (ob + 1)],
                                op0=ALU.mult, op1=ALU.add)
                assert c["pe"] == moe_pe[E - 1], (c["pe"], moe_pe)
                t_moe = c["dv"]

                pbs = (fpool[:, :], fpool2[:, :])
                out_toks = {}
                add_toks = {}
                w("sync", "dv", t_lgt_hl)
                for tc in range(NT):
                    pb = pbs[tc % 2]
                    if tc >= 2:
                        w("sync", "gp", add_toks[tc - 2])
                    tx = dma("d1", pb[:, 0:1024], x_tm[128 * tc:128 * (tc + 1), :])
                    w("gpsimd", "d1", tx)
                    w("gpsimd", "dv", moe_ev[7 * NT * 2 + tc * 2 + 1])
                    if tc >= 2:
                        w("gpsimd", "d3", out_toks[tc - 2])
                    add_toks[tc] = gps("tensor_tensor", pb[:, 1024:2048],
                                       accA[:, tc, :], pb[:, 0:1024], ALU.add)
                    w("scalar", "gp", add_toks[tc])
                    out_toks[tc] = dma("d3", y[128 * tc:128 * (tc + 1), :],
                                       pb[:, 1024:2048])
                w("scalar", "d3", out_toks[NT - 1])

            with nc.Block() as block:
                @block.tensor
                def _(eng):
                    program("tensor", eng)

                @block.vector
                def _(eng):
                    program("vector", eng)

                @block.scalar
                def _(eng):
                    program("scalar", eng)

                @block.sync
                def _(eng):
                    program("sync", eng)

                @block.gpsimd
                def _(eng):
                    program("gpsimd", eng)

    return nc


_NC_CACHE = None


def kernel(L0_attn_x, L0_attn_w_qkv, L0_attn_b_qkv, L0_attn_w_g1, L0_attn_b_g1,
           L0_attn_w_g2, L0_attn_b_g2, L0_attn_w_out, L0_attn_b_out, L0_attn_ln_g,
           L0_attn_ln_b, L0_ffn_Wr, L0_ffn_We):
    global _NC_CACHE
    x = np.asarray(L0_attn_x, np.float32)
    w_qkv = np.asarray(L0_attn_w_qkv, np.float32)
    b_qkv = np.asarray(L0_attn_b_qkv, np.float32)
    shared = {
        "wq_t": np.ascontiguousarray(w_qkv[0:D].T),
        "wk_t": np.ascontiguousarray(w_qkv[D:2 * D].T),
        "wv_t": np.ascontiguousarray(w_qkv[2 * D:3 * D].T),
        "wg1_t": np.ascontiguousarray(np.asarray(L0_attn_w_g1, np.float32).T),
        "wg2_t": np.ascontiguousarray(np.asarray(L0_attn_w_g2, np.float32).T),
        "wo_t": np.ascontiguousarray(np.asarray(L0_attn_w_out, np.float32).T),
        "wr_t": np.ascontiguousarray(np.asarray(L0_ffn_Wr, np.float32).T),
        "we": np.ascontiguousarray(np.asarray(L0_ffn_We, np.float32)),
        "bq": np.ascontiguousarray(b_qkv[0:D][None, :]),
        "bk": np.ascontiguousarray(b_qkv[D:2 * D][None, :]),
        "bv": np.ascontiguousarray(b_qkv[2 * D:3 * D][None, :]),
        "bg1": np.ascontiguousarray(np.asarray(L0_attn_b_g1, np.float32)[None, :]),
        "bg2": np.ascontiguousarray(np.asarray(L0_attn_b_g2, np.float32)[None, :]),
        "bo": np.ascontiguousarray(np.asarray(L0_attn_b_out, np.float32)[None, :]),
        "gam": np.ascontiguousarray(np.asarray(L0_attn_ln_g, np.float32).reshape(HC, 128).T),
        "bet": np.ascontiguousarray(np.asarray(L0_attn_ln_b, np.float32).reshape(HC, 128).T),
        "ltri": np.triu(np.ones((128, 128), np.float32), 1),
        "ident": np.eye(128, dtype=np.float32),
    }
    in_maps = []
    for b in range(8):
        m = dict(shared)
        m["x_fm"] = np.ascontiguousarray(x[b].T)
        m["x_tm"] = np.ascontiguousarray(x[b])
        in_maps.append(m)
    if _NC_CACHE is None:
        _NC_CACHE = build_nc()
    res = run_bass_kernel_spmd(_NC_CACHE, in_maps, core_ids=list(range(8)))
    return np.stack([np.asarray(r["y"], np.float32) for r in res.results], axis=0)
